# version 18
# baseline (speedup 1.0000x reference)
import numpy as np
from contextlib import ExitStack

import concourse.bass as bass
import concourse.mybir as mybir
from concourse.bass_utils import run_bass_kernel_spmd

F32 = mybir.dt.float32
F16 = mybir.dt.float16
I32 = mybir.dt.int32
I16 = mybir.dt.int16
I8 = mybir.dt.int8
AF = mybir.ActivationFunctionType
ALU = mybir.AluOpType

B, S, D = 4, 2048, 2048
TOK = 1024
NH, DH, DM = 16, 128, 64
IF = 5632
GUO = 2 * IF
WGU = 256
WDN = 704
NCH = 16
NTC = 8
KB = 1024

DEBUG = False
_RET_MAPS = [False]


class Sem:
    __slots__ = ("h", "n")

    def __init__(self, stk, nc, name):
        self.h = stk.enter_context(nc.semaphore(name))
        self.n = 0

    def inc(self, instr, k=1):
        instr.then_inc(self.h, k)
        self.n += k
        return self.n


def build():
    nc = bass.Bass("TRN2", target_bir_lowering=False)

    x_in = nc.dram_tensor("x_in", [TOK, D], F32, kind="ExternalInput").ap()
    wqT_d = nc.dram_tensor("wqT", [D, D], F32, kind="ExternalInput").ap()
    wkT_d = nc.dram_tensor("wkT", [D, D], F32, kind="ExternalInput").ap()
    wvT_d = nc.dram_tensor("wvT", [D, D], F32, kind="ExternalInput").ap()
    woT_d = nc.dram_tensor("woT", [D, D], F32, kind="ExternalInput").ap()
    bqT_d = nc.dram_tensor("bqT", [128, NH], F32, kind="ExternalInput").ap()
    bkT_d = nc.dram_tensor("bkT", [128, NH], F32, kind="ExternalInput").ap()
    bv_d = nc.dram_tensor("bv", [1, D], F32, kind="ExternalInput").ap()
    bo_d = nc.dram_tensor("bo", [1, D], F32, kind="ExternalInput").ap()
    wmqT_d = nc.dram_tensor("wmqT", [128, DM], F32, kind="ExternalInput").ap()
    wmkT_d = nc.dram_tensor("wmkT", [128, DM], F32, kind="ExternalInput").ap()
    gupT_d = nc.dram_tensor("gupT", [WGU, GUO], I32, kind="ExternalInput").ap()
    dnT_d = nc.dram_tensor("dnT", [WDN, D], I32, kind="ExternalInput").ap()
    sgu_d = [nc.dram_tensor("sgu%d" % w, [128, 2 * GUO], F16, kind="ExternalInput").ap()
             for w in range(2)]
    sdn_d = nc.dram_tensor("sdn", [6, 128, 2 * D], F16, kind="ExternalInput").ap()
    ident_d = nc.dram_tensor("ident", [128, 128], F16, kind="ExternalInput").ap()
    ones_d = nc.dram_tensor("onesr", [1, 128], F16, kind="ExternalInput").ap()
    mask_d = nc.dram_tensor("mask4p", [64, 4], F32, kind="ExternalInput").ap()

    y_out = nc.dram_tensor("y", [TOK, D], F32, kind="ExternalOutput").ap()

    kscr = "ExternalOutput" if DEBUG else "Internal"
    x1_dram = nc.dram_tensor("x1_dram", [TOK, D], F32, kind=kscr).ap()
    wd_dec = nc.dram_tensor("wd_dec", [24 * 128, 2 * D], F16, kind=kscr).ap()
    hT_dram = nc.dram_tensor("hT_dram", [48 * 128, TOK], F16, kind=kscr).ap()
    cc_in = nc.dram_tensor("cc_in", [64, 4 * NH * 129], F32, kind="Internal").ap()
    cc_out = nc.dram_tensor("cc_out", [64, 4 * NH * 129], F32, kind="Internal",
                            addr_space="Shared").ap()
    if DEBUG:
        dbg_kvfull = nc.dram_tensor("dbg_kvfull", [64, NH * 129], F32, kind="ExternalOutput").ap()
        dbg_ln = nc.dram_tensor("dbg_ln", [128, 32], F32, kind="ExternalOutput").ap()
        dbg_rbuf = nc.dram_tensor("dbg_rbuf", [128, D], F32, kind="ExternalOutput").ap()
        dbg_y2 = nc.dram_tensor("dbg_y2", [128, NTC * D], F32, kind="ExternalOutput").ap()
        dbg_kv = nc.dram_tensor("dbg_kv", [64, NH * 129], F32, kind="ExternalOutput").ap()
        dbg_att = nc.dram_tensor("dbg_att", [128, NH * TOK], F16, kind="ExternalOutput").ap()
        dbg_km = nc.dram_tensor("dbg_km", [128, NH * 512], F16, kind="ExternalOutput").ap()
        dbg_qm = nc.dram_tensor("dbg_qm", [64, NH * TOK], F16, kind="ExternalOutput").ap()

    TT, VV, AA, GG, YY = nc.tensor, nc.vector, nc.scalar, nc.gpsimd, nc.sync

    with ExitStack() as top:
        E = top.enter_context
        ident = E(nc.sbuf_tensor("identt", [128, 128], F16))
        onesr = E(nc.sbuf_tensor("onest", [1, 128], F16))
        onescol = E(nc.sbuf_tensor("onescol", [128, 1], F16))
        bqT = E(nc.sbuf_tensor("bqTt", [128, NH], F32))
        bkT = E(nc.sbuf_tensor("bkTt", [128, NH], F32))
        bv16 = E(nc.sbuf_tensor("bv16", [1, D], F16))
        bo16 = E(nc.sbuf_tensor("bo16", [1, D], F16))
        wmq16 = E(nc.sbuf_tensor("wmq16", [128, DM], F16))
        wmk16 = E(nc.sbuf_tensor("wmk16", [128, DM], F16))
        mask4p = E(nc.sbuf_tensor("mask4pt", [64, 4], F32))
        epsatt = E(nc.sbuf_tensor("epsatt", [1, 1], F32))
        EPSATT = 1e-6 * (2.0 ** -12)
        nc.const_aps.register(F32, EPSATT, epsatt[0:1, 0:1])
        epsln = E(nc.sbuf_tensor("epsln", [128, 1], F32))
        epsrms = E(nc.sbuf_tensor("epsrms", [128, 1], F32))
        stats = E(nc.sbuf_tensor("stats", [128, 24], F32))
        mv = E(nc.sbuf_tensor("mv", [128, 2], F32))
        mtmp = E(nc.sbuf_tensor("mtmp", [128, 4], F32))
        flushd = E(nc.sbuf_tensor("flushd", [128, 1], F32))
        pst = E(nc.sbuf_tensor("pst", [128, 64], F32))
        cln16 = E(nc.sbuf_tensor("cln16", [128, 1], F32))
        LN16 = float(-4.0 * np.log(2.0))
        nc.const_aps.register(F32, LN16, cln16[:, 0:1])
        arena = E(nc.sbuf_tensor("arena", [128, 192 * KB], I8))

        def av(off, dt, cols, parts=128):
            b = mybir.dt.size(dt) * cols
            return arena[0:parts, off:off + b].bitcast(dt)

        xT = av(0, F16, NCH * TOK)
        dnpk = av(32 * KB, I32, D)
        dnsc = av(40 * KB, F16, 2 * D)
        dnm = av(48 * KB, I16, 2 * D)
        dnsg = av(56 * KB, I16, 2 * D)
        dno = av(64 * KB, F16, 2 * D)
        elutmp = av(72 * KB, F32, KB)
        xh = [av(76 * KB, F16, D), av(80 * KB, F16, D)]
        kqT = [av(84 * KB, F16, 4 * TOK), av(92 * KB, F16, 4 * TOK)]
        km16 = av(100 * KB, F16, NH * 512)
        vgrp = av(116 * KB, F16, NTC * 512)
        ccpart = av(124 * KB, F32, NH * 129, 64)
        qm16 = av(100 * KB, F16, NH * TOK, 64)
        wsl = [av(133 * KB, F16, NCH * 512), av(149 * KB, F16, NCH * 512)]
        cctmp = av(164 * KB + 512, F32, NH * 129, 64)
        kvfull = av(173 * KB, F32, NH * 129, 64)
        kv16 = av(181 * KB + 512, F16, NH * 128, 64)
        ksum16 = av(185 * KB + 512, F16, NH, 64)
        rn = av(164 * KB + 512, F32, 2 * TOK, 1)
        rn16 = [av(186 * KB, F16, 2 * TOK, 1),
                av(96 * KB, F16, 2 * TOK, 1)]
        wofull = av(32 * KB, F16, NCH * D)
        attT = av(133 * KB, F16, NH * TOK)
        xrel = [av(100 * KB, F32, D), av(108 * KB, F32, D)]
        rbuf = av(116 * KB, F32, D)
        x1f = av(124 * KB, F32, D)
        x1h = av(96 * KB, F16, D)
        rstd4 = mtmp[:, 3:4]
        gpk = [av(165 * KB, I32, KB), av(169 * KB, I32, KB)]
        gsc = [av(173 * KB, F16, 2 * KB), av(177 * KB, F16, 2 * KB)]
        gbuf = av(32 * KB, F16, 8 * 2048)
        ubuf = av(64 * KB, F16, 8 * 2048)
        gsil = av(96 * KB, F16, 8 * KB)
        gtm = av(112 * KB, I16, 2 * KB)
        gts = av(116 * KB, I16, 2 * KB)
        gor = av(126 * KB, I16, 2 * KB)
        up16 = [av((120 + j) * KB, F16, 512) for j in range(4)]
        h16 = [av(124 * KB, F16, 512), av(125 * KB, F16, 512)]
        y2 = av(32 * KB, F32, NTC * D)
        hts = [av(96 * KB, F16, 8 * KB), av(112 * KB, F16, 8 * KB)]
        wds = [av(128 * KB, F16, 4 * KB), av(136 * KB, F16, 4 * KB)]
        r2full = av(96 * KB, F32, NTC * D)
        x1r = [av(160 * KB, F32, D), av(168 * KB, F32, D)]
        obuf = [av(176 * KB, F32, D), av(184 * KB, F32, D)]

        pb = [E(nc.psum_tensor("pb%d" % i, [128, 512], F32)) for i in range(8)]

        sx = Sem(top, nc, "sx")
        sg = Sem(top, nc, "sg")
        sw = [Sem(top, nc, "sw%d" % i) for i in range(2)]
        spk = [Sem(top, nc, "spk%d" % i) for i in range(2)]
        sdnl = Sem(top, nc, "sdnl")
        spe = Sem(top, nc, "spe")
        sdv = Sem(top, nc, "sdv")
        sac = Sem(top, nc, "sac")
        scc = Sem(top, nc, "scc")
        scin = Sem(top, nc, "scin")
        swd = Sem(top, nc, "swd")
        sht = Sem(top, nc, "sht")
        sx1 = Sem(top, nc, "sx1")
        sout = Sem(top, nc, "sout")
        sxr = [Sem(top, nc, "sxr%d" % i) for i in range(2)]
        shr = [Sem(top, nc, "shr%d" % i) for i in range(2)]
        swr = [Sem(top, nc, "swr%d" % i) for i in range(2)]

        bankW = [0] * 8
        bankR = [None] * 8

        def pe_acquire(i):
            if bankR[i] is not None:
                s, v = bankR[i]
                TT.wait_ge(s.h, v)
                bankR[i] = None

        def bank_done(i):
            bankW[i] = spe.n

        def rd_wait(eng, i):
            eng.wait_ge(spe.h, bankW[i])

        block = E(nc.Block())
        _scope_state = [None]

        def _scope(name):
            if _scope_state[0] is not None:
                n, sid = _scope_state[0]
                nc.leave_named_scope(n, sid, False)
                _scope_state[0] = None
            if name is not None:
                sid = nc.enter_named_scope(name, False)[0]
                _scope_state[0] = (name, sid)

        def body(_):
            YY.dma_start(out=ident[:], in_=ident_d).then_inc(sx.h, 16); sx.n += 16
            YY.dma_start(out=onesr[:], in_=ones_d).then_inc(sx.h, 16); sx.n += 16
            YY.dma_start(out=bqT[:], in_=bqT_d).then_inc(sx.h, 16); sx.n += 16
            YY.dma_start(out=bkT[:], in_=bkT_d).then_inc(sx.h, 16); sx.n += 16
            YY.dma_start(out=mask4p[:], in_=mask_d).then_inc(sx.h, 16); sx.n += 16
            consts_x = sx.n
            GG.dma_start(out=bv16[:], in_=bv_d).then_inc(sg.h, 16); sg.n += 16
            GG.dma_start(out=bo16[:], in_=bo_d).then_inc(sg.h, 16); sg.n += 16
            GG.dma_start(out=wmq16[:], in_=wmqT_d).then_inc(sg.h, 16); sg.n += 16
            GG.dma_start(out=wmk16[:], in_=wmkT_d).then_inc(sg.h, 16); sg.n += 16
            consts_g = sg.n
            VV.memset(onescol[:], 1.0)
            VV.memset(epsatt[:], 1e-6 * (2.0 ** -12))
            VV.memset(epsln[:], 1e-5)
            VV.memset(epsrms[:], 1e-6 * (2.0 ** -8))
            i = VV.memset(cln16[:], LN16)
            sdv.inc(i)

            dnstate = {"wcp": 0, "kp": 0, "last_dv": 0, "pk_rdy": 0}

            def dn_unit():
                wcp, kp = dnstate["wcp"], dnstate["kp"]
                if wcp >= 6:
                    return
                rows = 64 if wcp == 5 else 128
                if kp == 0:
                    YY.wait_ge(sdv.h, dnstate["last_dv"])
                    i = YY.dma_start(out=av(32 * KB, I32, D, rows),
                                     in_=dnT_d[wcp * 128:wcp * 128 + rows, :])
                    sdnl.inc(i, 16)
                    i = YY.dma_start(out=av(40 * KB, F16, 2 * D, rows), in_=sdn_d[wcp, 0:rows, :])
                    sdnl.inc(i, 16)
                    VV.wait_ge(sdnl.h, sdnl.n)
                    VV.tensor_scalar(out=av(40 * KB, F16, 2 * D, rows),
                                     in0=av(40 * KB, F16, 2 * D, rows),
                                     scalar1=float(2.0 ** 14), scalar2=None, op0=ALU.mult)
                    dnstate["pk_rdy"] = sdnl.n
                rows_ = rows
                pk16 = av(32 * KB, I32, D, rows_).bitcast(I16)
                tm = av(48 * KB, I16, 2 * D, rows_)
                tsg = av(56 * KB, I16, 2 * D, rows_)
                to = av(64 * KB, F16, 2 * D, rows_)
                sc = av(40 * KB, F16, 2 * D, rows_)
                sm = 9 - 4 * kp
                if sm >= 0:
                    VV.tensor_scalar(out=tm, in0=pk16, scalar1=sm, scalar2=0x0E00,
                                     op0=ALU.logical_shift_left, op1=ALU.bitwise_and)
                else:
                    VV.tensor_scalar(out=tm, in0=pk16, scalar1=-sm, scalar2=0x0E00,
                                     op0=ALU.logical_shift_right, op1=ALU.bitwise_and)
                ss = 12 - 4 * kp
                if ss > 0:
                    VV.tensor_scalar(out=tsg, in0=pk16, scalar1=ss, scalar2=-32768,
                                     op0=ALU.logical_shift_left, op1=ALU.bitwise_and)
                else:
                    VV.tensor_scalar(out=tsg, in0=pk16, scalar1=-32768, scalar2=None,
                                     op0=ALU.bitwise_and)
                VV.tensor_tensor(out=tm, in0=tm, in1=tsg, op=ALU.bitwise_or)
                VV.wait_ge(swd.h, swd.n)
                i = VV.tensor_tensor(out=to, in0=tm.bitcast(F16), in1=sc, op=ALU.mult)
                sdv.inc(i)
                dnstate["last_dv"] = sdv.n
                YY.wait_ge(sdv.h, sdv.n)
                r0 = (wcp * 4 + kp) * 128
                i = YY.dma_start(out=wd_dec[r0:r0 + rows_, :], in_=to)
                swd.inc(i, 16)
                if kp == 3:
                    dnstate["wcp"] += 1
                    dnstate["kp"] = 0
                else:
                    dnstate["kp"] = kp + 1

            _scope("P1_xT")
            xT3 = xT.rearrange("p (c t) -> p c t", c=NCH)
            xh_free = [0, 0]
            xh_rdy = [0, 0]
            for tc in range(NTC):
                par = tc % 2
                GG.wait_ge(spe.h, xh_free[par])
                i = GG.dma_start(out=xh[par], in_=x_in[tc * 128:(tc + 1) * 128, :])
                sg.inc(i, 16)
                xh_rdy[par] = sg.n
                TT.wait_ge(sg.h, xh_rdy[par])
                if tc == 0:
                    TT.wait_ge(sx.h, 16)
                for half in range(2):
                    bk = (2 * tc) % 4 + half
                    pe_acquire(bk)
                    pv = pb[bk][:].bitcast(F16)
                    for j in range(8):
                        ic = half * 8 + j
                        i = TT.transpose(pv[:, j * 128:(j + 1) * 128],
                                         xh[par][:, ic * 128:(ic + 1) * 128], ident[:])
                    spe.inc(i)
                    bank_done(bk)
                    eng, seng = (VV, sdv) if half == 0 else (AA, sac)
                    rd_wait(eng, bk)
                    dst = xT3[:, half * 8:(half + 1) * 8, tc * 128:(tc + 1) * 128]
                    src = pv[:, 0:1024].rearrange("p (c t) -> p c t", c=8)
                    if half == 0:
                        i = VV.tensor_copy(out=dst, in_=src)
                    else:
                        i = AA.copy(out=dst, in_=src)
                    seng.inc(i)
                    bankR[bk] = (seng, seng.n)
                xh_free[par] = spe.n
                dn_unit()
            xT_dv, xT_ac = sdv.n, sac.n

            slab_free = [0, 0]
            slab_rdy = [0, 0]
            kq_free = [0, 0]

            def emit_proj(wT_d, kind, biasT=None):
                first_call_wait = (kind == "k")
                for sb in range(4):
                    par = sb % 2
                    GG.wait_ge(spe.h, slab_free[par])
                    src = wT_d.rearrange("(c p) o -> p c o", p=128)[:, :, sb * 512:(sb + 1) * 512]
                    dstw = wsl[par].rearrange("p (c o) -> p c o", c=NCH)
                    i = GG.dma_start(out=dstw, in_=src)
                    sw[par].inc(i, 16)
                    slab_rdy[par] = sw[par].n

                    TT.wait_ge(sw[par].h, slab_rdy[par])
                    if sb == 0 and first_call_wait:
                        TT.wait_ge(sdv.h, xT_dv)
                        TT.wait_ge(sac.h, xT_ac)
                    if kind in ("k", "q"):
                        for ic in range(NCH):
                            for h4 in range(4):
                                for th2 in range(2):
                                    bk = h4 * 2 + th2
                                    if ic == 0:
                                        pe_acquire(bk)
                                    i = TT.matmul(
                                        pb[bk][:],
                                        wsl[par][:, ic * 512 + h4 * 128: ic * 512 + h4 * 128 + 128],
                                        xT[:, ic * TOK + th2 * 512: ic * TOK + th2 * 512 + 512],
                                        start=(ic == 0), stop=(ic == NCH - 1))
                        spe.inc(i)
                        for bk in range(8):
                            bank_done(bk)
                        slab_free[par] = spe.n
                        kqb = kqT[par]
                        VV.wait_ge(spe.h, spe.n)
                        VV.wait_ge(spe.h, kq_free[par])
                        if sb == 0:
                            VV.wait_ge(sx.h, consts_x)
                        for h4 in range(4):
                            for th2 in range(2):
                                bk = h4 * 2 + th2
                                i = VV.tensor_scalar(
                                    out=kqb[:, h4 * TOK + th2 * 512: h4 * TOK + th2 * 512 + 512],
                                    in0=pb[bk][:], scalar1=biasT[:, sb * 4 + h4: sb * 4 + h4 + 1],
                                    scalar2=None, op0=ALU.add)
                                sdv.inc(i)
                                bankR[bk] = (sdv, sdv.n)
                        evac_dv = sdv.n
                        wmap = wmk16 if kind == "k" else wmq16
                        TT.wait_ge(sdv.h, evac_dv)
                        if sb == 0 and kind == "k":
                            TT.wait_ge(sg.h, consts_g)
                        for h4 in range(4):
                            h = sb * 4 + h4
                            if kind == "k":
                                bk = h4 * 2
                                pe_acquire(bk)
                                for tcc in range(NTC):
                                    i = TT.matmul(
                                        pb[bk][:, tcc * 64:(tcc + 1) * 64],
                                        kqb[:, h4 * TOK + tcc * 128: h4 * TOK + tcc * 128 + 128],
                                        wmap[:], start=(tcc == 0), stop=(tcc == NTC - 1),
                                        skip_group_check=True)
                                spe.inc(i)
                                bank_done(bk)
                                rd_wait(VV, bk)
                                i = VV.tensor_scalar(out=elutmp[:, 0:512], in0=pb[bk][:],
                                                     scalar1=0.0, scalar2=None, op0=ALU.min)
                                sdv.inc(i)
                                rd_wait(AA, bk)
                                AA.wait_ge(sdv.h, sdv.n)
                                i = AA.activation(out=elutmp[:, 512:1024], in_=pb[bk][:], func=AF.Relu)
                                sac.inc(i)
                                bankR[bk] = (sac, sac.n)
                                i = AA.activation(out=elutmp[:, 0:512], in_=elutmp[:, 0:512], func=AF.Exp)
                                sac.inc(i)
                                VV.wait_ge(sac.h, sac.n)
                                i = VV.tensor_tensor(out=km16[:, h * 512:(h + 1) * 512],
                                                     in0=elutmp[:, 0:512], in1=elutmp[:, 512:1024],
                                                     op=ALU.add)
                                sdv.inc(i)
                            else:
                                for th2 in range(2):
                                    bk = h4 * 2 + th2
                                    pe_acquire(bk)
                                    i = TT.matmul(
                                        pb[bk][0:64, :], wmap[:],
                                        kqb[:, h4 * TOK + th2 * 512: h4 * TOK + th2 * 512 + 512],
                                        start=True, stop=True)
                                    spe.inc(i)
                                    bank_done(bk)
                                    rd_wait(VV, bk)
                                    i = VV.tensor_scalar(out=elutmp[0:64, 0:512], in0=pb[bk][0:64, :],
                                                         scalar1=0.0, scalar2=None, op0=ALU.min)
                                    sdv.inc(i)
                                    rd_wait(AA, bk)
                                    AA.wait_ge(sdv.h, sdv.n)
                                    i = AA.activation(out=elutmp[0:64, 512:1024], in_=pb[bk][0:64, :],
                                                      func=AF.Relu)
                                    sac.inc(i)
                                    bankR[bk] = (sac, sac.n)
                                    i = AA.activation(out=elutmp[0:64, 0:512], in_=elutmp[0:64, 0:512],
                                                      func=AF.Exp)
                                    sac.inc(i)
                                    VV.wait_ge(sac.h, sac.n)
                                    i = VV.tensor_tensor(
                                        out=qm16[:, h * TOK + th2 * 512: h * TOK + th2 * 512 + 512],
                                        in0=elutmp[0:64, 0:512], in1=elutmp[0:64, 512:1024],
                                        op=ALU.add)
                                    sdv.inc(i)
                        kq_free[par] = spe.n
                        dn_unit()
                    else:
                        if sb == 0:
                            TT.wait_ge(sx.h, consts_x)
                            TT.wait_ge(sg.h, consts_g)
                            TT.wait_ge(sdv.h, sdv.n)
                        pe_acquire(4)
                        pe_acquire(5)
                        for tcg in range(2):
                            for ic in range(NCH):
                                for t4 in range(4):
                                    tcc = tcg * 4 + t4
                                    bk = t4
                                    if ic == 0:
                                        pe_acquire(bk)
                                    i = TT.matmul(
                                        pb[bk][:],
                                        xT[:, ic * TOK + tcc * 128: ic * TOK + tcc * 128 + 128],
                                        wsl[par][:, ic * 512: ic * 512 + 512],
                                        start=(ic == 0), stop=False)
                            for t4 in range(4):
                                i = TT.matmul(pb[t4][:], onesr[0:1, 0:128],
                                              bv16[0:1, sb * 512:(sb + 1) * 512],
                                              start=False, stop=True)
                            spe.inc(i)
                            for t4 in range(4):
                                bank_done(t4)
                            AA.wait_ge(spe.h, spe.n)
                            for t4 in range(4):
                                tcc = tcg * 4 + t4
                                i = AA.copy(out=vgrp[:, tcc * 512:(tcc + 1) * 512], in_=pb[t4][:])
                                sac.inc(i)
                                bankR[t4] = (sac, sac.n)
                            TT.wait_ge(sac.h, sac.n)
                            for h4 in range(4):
                                h = sb * 4 + h4
                                for t4 in range(4):
                                    tcc = tcg * 4 + t4
                                    st = (tcg == 0 and h4 == 0 and t4 == 0)
                                    sp = (tcg == 1 and h4 == 3 and t4 == 3)
                                    i = TT.matmul(
                                        pb[4][0:64, h4 * 128:(h4 + 1) * 128],
                                        km16[:, h * 512 + tcc * 64: h * 512 + tcc * 64 + 64],
                                        vgrp[:, tcc * 512 + h4 * 128: tcc * 512 + h4 * 128 + 128],
                                        start=st, stop=sp, skip_group_check=True)
                                    i = TT.matmul(
                                        pb[5][0:64, h:h + 1],
                                        km16[:, h * 512 + tcc * 64: h * 512 + tcc * 64 + 64],
                                        onescol[:], start=st, stop=sp, skip_group_check=True)
                        spe.inc(i)
                        bank_done(4)
                        bank_done(5)
                        slab_free[par] = spe.n
                        VV.wait_ge(spe.h, spe.n)
                        for h4 in range(4):
                            h = sb * 4 + h4
                            i = VV.tensor_copy(out=ccpart[:, h * 129: h * 129 + 128],
                                               in_=pb[4][0:64, h4 * 128:(h4 + 1) * 128])
                            sdv.inc(i)
                            i = VV.tensor_copy(out=ccpart[:, h * 129 + 128: h * 129 + 129],
                                               in_=pb[5][0:64, h:h + 1])
                            sdv.inc(i)
                        bankR[4] = (sdv, sdv.n)
                        bankR[5] = (sdv, sdv.n)
                        dn_unit()
                        dn_unit()

            _scope("P2_K")
            emit_proj(wkT_d, "k", biasT=bkT)
            if DEBUG:
                YY.wait_ge(sdv.h, sdv.n)
                i = YY.dma_start(out=dbg_km, in_=km16); sout.inc(i, 16)
            _scope("P3_V")
            emit_proj(wvT_d, "v")
            while dnstate["wcp"] < 6:
                dn_unit()
            _scope("P4_cc")
            for b in range(4):
                i = VV.tensor_scalar(out=cctmp, in0=ccpart, scalar1=mask4p[0:64, b:b + 1],
                                     scalar2=None, op0=ALU.mult)
                sdv.inc(i)
                YY.wait_ge(sdv.h, sdv.n)
                i = YY.dma_start(out=cc_in[:, b * NH * 129:(b + 1) * NH * 129], in_=cctmp)
                scin.inc(i, 16)
                if b < 3:
                    VV.wait_ge(scin.h, scin.n)
            GG.wait_ge(scin.h, scin.n)
            i = GG.collective_compute("AllReduce", ALU.add,
                                      replica_groups=[[0, 1, 2, 3, 4, 5, 6, 7]],
                                      ins=[cc_in.opt()], outs=[cc_out.opt()])
            i.then_inc(scc.h)
            scc.n = 1
            if DEBUG:
                YY.wait_ge(sdv.h, sdv.n)
                i = YY.dma_start(out=dbg_kv, in_=ccpart); sout.inc(i, 16)

            _scope("P5_Q")
            emit_proj(wqT_d, "q", biasT=bqT)
            qm_dv = sdv.n
            if DEBUG:
                YY.wait_ge(sdv.h, qm_dv)
                i = YY.dma_start(out=dbg_qm, in_=qm16); sout.inc(i, 16)
                VV.wait_ge(sout.h, sout.n)

            GG.wait_ge(spe.h, kq_free[0])
            GG.wait_ge(spe.h, kq_free[1])
            GG.wait_ge(sdv.h, qm_dv)
            GG.wait_ge(sac.h, sac.n)
            src = woT_d.rearrange("(c p) o -> p c o", p=128)
            i = GG.dma_start(out=wofull.rearrange("p (c o) -> p c o", c=NCH), in_=src)
            sg.inc(i, 16)
            wo_rdy = sg.n

            _scope("P6_att")
            GG.wait_ge(scc.h, 1)
            for b in range(4):
                GG.wait_ge(sdv.h, sdv.n)
                i = GG.dma_start(out=cctmp, in_=cc_out[:, b * NH * 129:(b + 1) * NH * 129])
                sg.inc(i, 16)
                VV.wait_ge(sg.h, sg.n)
                if b == 0:
                    i = VV.tensor_scalar(out=kvfull, in0=cctmp, scalar1=mask4p[0:64, 0:1],
                                         scalar2=None, op0=ALU.mult)
                else:
                    i = VV.scalar_tensor_tensor(out=kvfull, in0=cctmp,
                                                scalar=mask4p[0:64, b:b + 1], in1=kvfull,
                                                op0=ALU.mult, op1=ALU.add)
                sdv.inc(i)
            for h in range(NH):
                VV.tensor_scalar(out=kv16[:, h * 128:(h + 1) * 128],
                                 in0=kvfull[:, h * 129:h * 129 + 128],
                                 scalar1=float(2.0 ** -12), scalar2=None, op0=ALU.mult)
            i = VV.tensor_copy(out=ksum16, in_=kvfull[:, 128:NH * 129:129])
            sdv.inc(i)
            kv_rdy = sdv.n
            if DEBUG:
                YY.wait_ge(sdv.h, kv_rdy)
                i = YY.dma_start(out=dbg_kvfull, in_=kvfull); sout.inc(i, 16)

            rn_dv = [0] * 8
            qmn_dv = [0] * 8
            TT.wait_ge(sdv.h, kv_rdy)
            NG = 8
            for s in range(NG + 3):
                g = s - 3
                if 0 <= g < NG:
                    h0 = g * 2
                    TT.wait_ge(sdv.h, qmn_dv[g])
                    for h4 in range(2):
                        h = h0 + h4
                        for th2 in range(2):
                            bk = 4 + h4 * 2 + th2
                            pe_acquire(bk)
                            i = TT.matmul(pb[bk][:], kv16[:, h * 128:(h + 1) * 128],
                                          qm16[:, h * TOK + th2 * 512: h * TOK + th2 * 512 + 512],
                                          start=True, stop=True)
                            spe.inc(i)
                            bank_done(bk)
                    AA.wait_ge(spe.h, spe.n)
                    for h4 in range(2):
                        h = h0 + h4
                        for th2 in range(2):
                            bk = 4 + h4 * 2 + th2
                            i = AA.copy(out=attT[:, h * TOK + th2 * 512: h * TOK + th2 * 512 + 512],
                                        in_=pb[bk][:])
                            sac.inc(i)
                            bankR[bk] = (sac, sac.n)
                g = s - 2
                if 0 <= g < NG:
                    p2 = g % 2
                    TT.wait_ge(sdv.h, rn_dv[g])
                    for h4 in range(2):
                        bk = 2 + h4
                        pe_acquire(bk)
                        for th2 in range(2):
                            i = TT.matmul(pb[bk][64 * th2:64 * th2 + 64, :], onesr[0:1, 0:64],
                                          rn16[p2][0:1, (h4 * 2 + th2) * 512:(h4 * 2 + th2) * 512 + 512],
                                          start=True, stop=True, skip_group_check=True)
                        spe.inc(i)
                        bank_done(bk)
                    rd_wait(VV, 2)
                    rd_wait(VV, 3)
                    h0 = g * 2
                    for h4 in range(2):
                        h = h0 + h4
                        for th2 in range(2):
                            i = VV.tensor_tensor(
                                out=qm16[:, h * TOK + th2 * 512: h * TOK + th2 * 512 + 512],
                                in0=qm16[:, h * TOK + th2 * 512: h * TOK + th2 * 512 + 512],
                                in1=pb[2 + h4][64 * th2:64 * th2 + 64, :], op=ALU.mult)
                            sdv.inc(i)
                    bankR[2] = (sdv, sdv.n)
                    bankR[3] = (sdv, sdv.n)
                    qmn_dv[g] = sdv.n
                g = s - 1
                if 0 <= g < NG:
                    p2 = g % 2
                    rd_wait(AA, 0)
                    rd_wait(AA, 1)
                    if g >= 1:
                        AA.wait_ge(sdv.h, rn_dv[g - 1])
                    for h4 in range(2):
                        for th2 in range(2):
                            i = AA.activation(
                                out=rn[0:1, (h4 * 2 + th2) * 512:(h4 * 2 + th2) * 512 + 512],
                                in_=pb[h4][64 * th2:64 * th2 + 1, :],
                                func=AF.Copy, scale=float(2.0 ** -12), bias=EPSATT)
                            sac.inc(i)
                    bankR[0] = (sac, sac.n)
                    bankR[1] = (sac, sac.n)
                    VV.wait_ge(sac.h, sac.n)
                    VV.reciprocal(out=rn[0:1, :], in_=rn[0:1, :])
                    i = VV.tensor_copy(out=rn16[p2][0:1, :], in_=rn[0:1, :])
                    sdv.inc(i)
                    rn_dv[g] = sdv.n
                g = s
                if g < NG:
                    h0 = g * 2
                    for h4 in range(2):
                        h = h0 + h4
                        bk = h4
                        pe_acquire(bk)
                        for th2 in range(2):
                            i = TT.matmul(pb[bk][64 * th2:64 * th2 + 1, :], ksum16[:, h:h + 1],
                                          qm16[:, h * TOK + th2 * 512: h * TOK + th2 * 512 + 512],
                                          start=True, stop=True, skip_group_check=True)
                        spe.inc(i)
                        bank_done(bk)
            att_ac = sac.n
            if DEBUG:
                YY.wait_ge(sac.h, att_ac)
                i = YY.dma_start(out=dbg_att, in_=attT); sout.inc(i, 16)

            _scope("P7_Wo")
            x1T3 = xT.rearrange("p (c t) -> p c t", c=NCH)
            TT.wait_ge(sg.h, wo_rdy)
            TT.wait_ge(sac.h, att_ac)
            YY.wait_ge(spe.h, spe.n)
            xrel_free = [0, 0]
            x1h_pe = 0
            for tc in range(NTC):
                par = tc % 2
                YY.wait_ge(sdv.h, xrel_free[par])
                i = YY.dma_start(out=xrel[par], in_=x_in[tc * 128:(tc + 1) * 128, :])
                sxr[par].inc(i, 16)
                for os in range(4):
                    bk = os
                    pe_acquire(bk)
                    for hh in range(NH):
                        i = TT.matmul(pb[bk][:],
                                      attT[:, hh * TOK + tc * 128: hh * TOK + tc * 128 + 128],
                                      wofull[:, hh * D + os * 512: hh * D + os * 512 + 512],
                                      start=(hh == 0), stop=False)
                    i = TT.matmul(pb[bk][:], onesr[0:1, 0:128],
                                  bo16[0:1, os * 512:(os + 1) * 512], start=False, stop=True)
                    spe.inc(i)
                    bank_done(bk)
                VV.wait_ge(sxr[par].h, sxr[par].n)
                VV.wait_ge(spe.h, spe.n)
                for os in range(4):
                    i = VV.tensor_tensor(out=rbuf[:, os * 512:(os + 1) * 512],
                                         in0=xrel[par][:, os * 512:(os + 1) * 512],
                                         in1=pb[os][:], op=ALU.add)
                    sdv.inc(i)
                    bankR[os] = (sdv, sdv.n)
                xrel_free[par] = sdv.n
                AA.wait_ge(sdv.h, sdv.n)
                AA.wait_ge(sx1.h, sx1.n)
                AA.activation(out=x1f, in_=rbuf, func=AF.Square, accum_out=stats[:, 8:9])
                AA.activation(out=x1f, in_=rbuf, func=AF.Identity, accum_out=stats[:, 9:10])
                AA.activation(out=flushd[:], in_=stats[:, 8:9], func=AF.Copy)
                AA.activation(out=flushd[:], in_=stats[:, 9:10], func=AF.Copy)
                AA.activation(out=mv[:, 0:1], in_=stats[:, 9:10], func=AF.Identity, scale=1.0 / 2048)
                AA.activation(out=mtmp[:, 3:4], in_=stats[:, 8:9], func=AF.Identity, scale=1.0 / 2048)
                i = AA.activation(out=mtmp[:, 2:3], in_=mv[:, 0:1], func=AF.Square)
                sac.inc(i)
                VV.wait_ge(sac.h, sac.n)
                i = VV.scalar_tensor_tensor(out=mv[:, 1:2], in0=mtmp[:, 2:3], scalar=-1.0,
                                            in1=mtmp[:, 3:4], op0=ALU.mult, op1=ALU.add)
                sdv.inc(i)
                AA.wait_ge(sdv.h, sdv.n)
                AA.activation(out=mtmp[:, 0:1], in_=mv[:, 1:2], func=AF.Ln, bias=epsln[:, 0:1])
                AA.activation(out=mtmp[:, 1:2], in_=mtmp[:, 0:1], func=AF.Exp, scale=-0.5)
                i = AA.activation(out=rstd4, in_=mtmp[:, 0:1], func=AF.Exp, scale=-0.5, bias=LN16)
                sac.inc(i)
                VV.wait_ge(sac.h, sac.n)
                VV.wait_ge(sx1.h, sx1.n)
                i = VV.tensor_scalar(out=x1f, in0=rbuf, scalar1=mv[:, 0:1],
                                     scalar2=mtmp[:, 1:2], op0=ALU.subtract, op1=ALU.mult)
                sdv.inc(i)
                YY.wait_ge(sdv.h, sdv.n)
                i = YY.dma_start(out=x1_dram[tc * 128:(tc + 1) * 128, :], in_=x1f)
                sx1.inc(i, 16)
                VV.wait_ge(spe.h, x1h_pe)
                i = VV.tensor_scalar(out=x1h, in0=rbuf, scalar1=mv[:, 0:1],
                                     scalar2=rstd4, op0=ALU.subtract, op1=ALU.mult)
                sdv.inc(i)
                x1h_dv = sdv.n
                TT.wait_ge(sdv.h, x1h_dv)
                for half in range(2):
                    bk = 4 + half
                    pe_acquire(bk)
                    pv = pb[bk][:].bitcast(F16)
                    for j in range(8):
                        kidx = half * 8 + j
                        wc, kk = kidx // 8, kidx % 8
                        col0 = 1024 * wc + kk
                        i = TT.transpose(pv[:, j * 128:(j + 1) * 128],
                                         x1h[:, col0:col0 + 1017:8], ident[:])
                    spe.inc(i)
                    bank_done(bk)
                    eng, seng = (VV, sdv) if half == 0 else (AA, sac)
                    rd_wait(eng, bk)
                    dst = x1T3[:, half * 8:(half + 1) * 8, tc * 128:(tc + 1) * 128]
                    srcp = pv[:, 0:1024].rearrange("p (c t) -> p c t", c=8)
                    if half == 0:
                        i = VV.tensor_copy(out=dst, in_=srcp)
                    else:
                        i = AA.copy(out=dst, in_=srcp)
                    seng.inc(i)
                    bankR[bk] = (seng, seng.n)
                x1h_pe = spe.n
                if DEBUG and tc == 7:
                    YY.wait_ge(sdv.h, sdv.n)
                    i = YY.dma_start(out=dbg_ln[:, 0:24], in_=stats[:]); sout.inc(i, 16)
                    i = YY.dma_start(out=dbg_ln[:, 24:26], in_=mv[:]); sout.inc(i, 16)
                    i = YY.dma_start(out=dbg_ln[:, 26:30], in_=mtmp[:]); sout.inc(i, 16)
                    i = YY.dma_start(out=dbg_rbuf, in_=rbuf); sout.inc(i, 16)
            x1T_dv, x1T_ac = sdv.n, sac.n

            _scope("P8_gu")
            ldp = [0]
            gpk_free = [0, 0]

            def decode_range(dstbuf, obase, owidth):
                for wc in range(2):
                    par = ldp[0] % 2
                    ldp[0] += 1
                    YY.wait_ge(sdv.h, gpk_free[par])
                    i = YY.dma_start(out=gpk[par][:, 0:owidth],
                                     in_=gupT_d[wc * 128:(wc + 1) * 128, obase:obase + owidth])
                    spk[par].inc(i, 16)
                    i = YY.dma_start(out=gsc[par][:, 0:2 * owidth],
                                     in_=sgu_d[wc][:, 2 * obase:2 * obase + 2 * owidth])
                    spk[par].inc(i, 16)
                    VV.wait_ge(spk[par].h, spk[par].n)
                    VV.tensor_scalar(out=gsc[par][:, 0:2 * owidth], in0=gsc[par][:, 0:2 * owidth],
                                     scalar1=float(2.0 ** 14), scalar2=None, op0=ALU.mult)
                    pk16 = gpk[par][:, 0:owidth].bitcast(I16)
                    for kp in range(4):
                        sm = 9 - 4 * kp
                        if sm >= 0:
                            VV.tensor_scalar(out=gtm[:, 0:2 * owidth], in0=pk16, scalar1=sm,
                                             scalar2=0x0E00, op0=ALU.logical_shift_left,
                                             op1=ALU.bitwise_and)
                        else:
                            VV.tensor_scalar(out=gtm[:, 0:2 * owidth], in0=pk16, scalar1=-sm,
                                             scalar2=0x0E00, op0=ALU.logical_shift_right,
                                             op1=ALU.bitwise_and)
                        ss = 12 - 4 * kp
                        if ss > 0:
                            VV.tensor_scalar(out=gts[:, 0:2 * owidth], in0=pk16, scalar1=ss,
                                             scalar2=-32768, op0=ALU.logical_shift_left,
                                             op1=ALU.bitwise_and)
                        else:
                            VV.tensor_scalar(out=gts[:, 0:2 * owidth], in0=pk16, scalar1=-32768,
                                             scalar2=None, op0=ALU.bitwise_and)
                        VV.tensor_tensor(out=gtm[:, 0:2 * owidth], in0=gtm[:, 0:2 * owidth],
                                         in1=gts[:, 0:2 * owidth], op=ALU.bitwise_or)
                        ti = wc * 4 + kp
                        dsc = dstbuf[:, ti * 2048: ti * 2048 + 2 * owidth].rearrange(
                            "p (l k j) -> p j k l", l=2, k=8)
                        i = VV.tensor_tensor(out=dsc,
                                             in0=gtm[:, 0:2 * owidth].bitcast(F16),
                                             in1=gsc[par][:, 0:2 * owidth], op=ALU.mult)
                        sdv.inc(i)
                    gpk_free[par] = sdv.n
                return sdv.n

            gbuf_pe = 0
            ubuf_pe = 0
            h16_free = [0, 0]
            hm_dv = [0] * 16
            up16_free = [0] * 4
            OWS = [1024, 1024, 1024, 1024, 1024, 512]
            g_rdy = [0] * 6
            u_rdy = [0] * 6
            g_rdy[0] = decode_range(gbuf, 0, OWS[0])
            u_rdy[0] = decode_range(ubuf, IF, OWS[0])
            for jc in range(6):
                ow = OWS[jc]
                M = 64 if jc == 5 else 128
                TT.wait_ge(sdv.h, g_rdy[jc])
                if jc == 0:
                    TT.wait_ge(sdv.h, x1T_dv)
                    TT.wait_ge(sac.h, x1T_ac)
                for kq in range(8):
                    for th in range(2):
                        bk = (kq * 2 + th) % 4
                        pe_acquire(bk)
                        first = True
                        for wc in range(2):
                            for kp in range(4):
                                for ln in range(2):
                                    ti = wc * 4 + kp
                                    cix = wc * 8 + kp + 4 * ln
                                    st0 = ti * 2048 + (ln * 8 + kq) * M
                                    i = TT.matmul(
                                        pb[bk][0:M, :],
                                        gbuf[:, st0: st0 + M],
                                        xT[:, cix * TOK + th * 512: cix * TOK + th * 512 + 512],
                                        start=first, stop=(wc == 1 and kp == 3 and ln == 1))
                                    first = False
                        spe.inc(i)
                        bank_done(bk)
                        rd_wait(AA, bk)
                        AA.wait_ge(sdv.h, hm_dv[kq * 2 + th])
                        i = AA.activation(
                            out=gsil[0:M, kq * 1024 + th * 512: kq * 1024 + th * 512 + 512],
                            in_=pb[bk][0:M, :], func=AF.Silu, scale=16.0)
                        sac.inc(i)
                        bankR[bk] = (sac, sac.n)
                gbuf_pe = spe.n
                if jc < 5:
                    VV.wait_ge(spe.h, gbuf_pe)
                    g_rdy[jc + 1] = decode_range(gbuf, (jc + 1) * 1024, OWS[jc + 1])
                TT.wait_ge(sdv.h, u_rdy[jc])
                for kq in range(8):
                    ci = jc * 8 + kq
                    for th in range(2):
                        bk = 4 + (kq * 2 + th) % 4
                        pe_acquire(bk)
                        first = True
                        for wc in range(2):
                            for kp in range(4):
                                for ln in range(2):
                                    ti = wc * 4 + kp
                                    cix = wc * 8 + kp + 4 * ln
                                    st0 = ti * 2048 + (ln * 8 + kq) * M
                                    i = TT.matmul(
                                        pb[bk][0:M, :],
                                        ubuf[:, st0: st0 + M],
                                        xT[:, cix * TOK + th * 512: cix * TOK + th * 512 + 512],
                                        start=first, stop=(wc == 1 and kp == 3 and ln == 1))
                                    first = False
                        spe.inc(i)
                        bank_done(bk)
                        rd_wait(AA, bk)
                        sl4 = (kq * 2 + th) % 4
                        AA.wait_ge(sdv.h, up16_free[sl4])
                        i = AA.copy(out=up16[sl4][0:M, :], in_=pb[bk][0:M, :])
                        sac.inc(i)
                        bankR[bk] = (sac, sac.n)
                        hp = th
                        VV.wait_ge(sac.h, sac.n)
                        VV.wait_ge(sht.h, h16_free[hp])
                        i = VV.tensor_tensor(
                            out=h16[hp][0:M, :],
                            in0=gsil[0:M, kq * 1024 + th * 512: kq * 1024 + th * 512 + 512],
                            in1=up16[sl4][0:M, :], op=ALU.mult)
                        sdv.inc(i)
                        hm_dv[kq * 2 + th] = sdv.n
                        up16_free[sl4] = sdv.n
                        YY.wait_ge(sdv.h, sdv.n)
                        i = YY.dma_start(
                            out=hT_dram[ci * 128: ci * 128 + M, th * 512:(th + 1) * 512],
                            in_=h16[hp][0:M, :])
                        sht.inc(i, 16)
                        h16_free[hp] = sht.n
                ubuf_pe = spe.n
                if jc < 5:
                    VV.wait_ge(spe.h, ubuf_pe)
                    u_rdy[jc + 1] = decode_range(ubuf, IF + (jc + 1) * 1024, OWS[jc + 1])

            YY.wait_ge(swd.h, swd.n)
            hts_free = [spe.n, spe.n]
            wds_free = [spe.n, spe.n]
            ht_all = sht.n
            for o2s in range(4):
                nci = 0
                for wcp in range(6):
                    rows = 64 if wcp == 5 else 128
                    p2 = wcp % 2
                    YY.wait_ge(spe.h, wds_free[p2])
                    srcw = wd_dec[wcp * 512:(wcp + 1) * 512,
                                  o2s * 1024:(o2s + 1) * 1024].rearrange("(c p) o -> p c o", p=128)
                    i = YY.dma_start(out=wds[p2].rearrange("p (c o) -> p c o", c=4), in_=srcw)
                    swr[p2].inc(i, 16)
                    GG.wait_ge(spe.h, hts_free[p2])
                    srch = hT_dram[wcp * 1024:(wcp + 1) * 1024, :].rearrange("(c p) t -> p c t", p=128)
                    i = GG.dma_start(out=hts[p2].rearrange("p (c t) -> p c t", c=8), in_=srch)
                    shr[p2].inc(i, 16)
                    TT.wait_ge(shr[p2].h, shr[p2].n)
                    TT.wait_ge(swr[p2].h, swr[p2].n)
                    if o2s == 0 and wcp == 0:
                        TT.wait_ge(sht.h, ht_all)
                    for kp in range(4):
                        for ln in range(2):
                            kk = kp + 4 * ln
                            for tcc in range(NTC):
                                bk = tcc
                                if nci == 0:
                                    pe_acquire(bk)
                                i = TT.matmul(
                                    pb[bk][:],
                                    hts[p2][0:rows, kk * 1024 + tcc * 128: kk * 1024 + tcc * 128 + 128],
                                    wds[p2][0:rows, kp * 1024 + ln: kp * 1024 + 1024: 2],
                                    start=(nci == 0), stop=(nci == 47),
                                    skip_group_check=True)
                            nci += 1
                    spe.inc(i)
                    hts_free[p2] = spe.n
                    wds_free[p2] = spe.n
                for tcc in range(NTC):
                    bank_done(tcc)
                AA.wait_ge(spe.h, spe.n)
                VV.wait_ge(spe.h, spe.n)
                for tcc in range(NTC):
                    if tcc % 2 == 0:
                        i = AA.copy(out=y2[:, tcc * D + o2s * 512: tcc * D + o2s * 512 + 512],
                                    in_=pb[tcc][:])
                        sac.inc(i)
                        bankR[tcc] = (sac, sac.n)
                    else:
                        i = VV.tensor_copy(out=y2[:, tcc * D + o2s * 512: tcc * D + o2s * 512 + 512],
                                           in_=pb[tcc][:])
                        sdv.inc(i)
                        bankR[tcc] = (sdv, sdv.n)
            y2_dv, y2_ac = sdv.n, sac.n
            if DEBUG:
                YY.wait_ge(sdv.h, y2_dv)
                YY.wait_ge(sac.h, y2_ac)
                i = YY.dma_start(out=dbg_y2, in_=y2); sout.inc(i, 16)

            _scope("P10_ln2")
            def rf(t):
                return r2full[:, t * D:(t + 1) * D]

            YY.wait_ge(sx1.h, sx1.n)
            obuf_free = [0, 0]
            x1r_free = [0, 0]
            AA.wait_ge(sdv.h, y2_dv)
            for tcc in range(NTC):
                AA.activation(out=rf(tcc), in_=y2[:, tcc * D:(tcc + 1) * D], func=AF.Square,
                              accum_out=stats[:, tcc:tcc + 1])
            AA.activation(out=flushd[:], in_=stats[:, 7:8], func=AF.Copy)
            AA.activation(out=flushd[:], in_=stats[:, 7:8], func=AF.Copy)
            AA.activation(out=pst[:, 0:8], in_=stats[:, 0:8], func=AF.Identity, scale=1.0 / 2048)
            AA.activation(out=pst[:, 0:8], in_=pst[:, 0:8], func=AF.Ln, bias=epsrms[:, 0:1])
            i = AA.activation(out=pst[:, 0:8], in_=pst[:, 0:8], func=AF.Exp, scale=-0.5)
            sac.inc(i)
            VV.wait_ge(sac.h, sac.n)
            for tcc in range(NTC):
                par = tcc % 2
                YY.wait_ge(sdv.h, x1r_free[par])
                i = YY.dma_start(out=x1r[par], in_=x1_dram[tcc * 128:(tcc + 1) * 128, :])
                sxr[par].inc(i, 16)
                VV.wait_ge(sxr[par].h, sxr[par].n)
                i = VV.scalar_tensor_tensor(out=rf(tcc), in0=y2[:, tcc * D:(tcc + 1) * D],
                                            scalar=pst[:, tcc:tcc + 1], in1=x1r[par],
                                            op0=ALU.mult, op1=ALU.add)
                sdv.inc(i)
                x1r_free[par] = sdv.n
            r2_dv = sdv.n
            AA.wait_ge(sdv.h, r2_dv)
            for tcc in range(NTC):
                AA.activation(out=obuf[0], in_=rf(tcc), func=AF.Square,
                              accum_out=stats[:, 8 + tcc:9 + tcc])
                AA.activation(out=obuf[0], in_=rf(tcc), func=AF.Identity,
                              accum_out=stats[:, 16 + tcc:17 + tcc])
            AA.activation(out=flushd[:], in_=stats[:, 15:16], func=AF.Copy)
            AA.activation(out=flushd[:], in_=stats[:, 23:24], func=AF.Copy)
            AA.activation(out=pst[:, 8:16], in_=stats[:, 16:24], func=AF.Identity, scale=1.0 / 2048)
            AA.activation(out=pst[:, 16:24], in_=stats[:, 8:16], func=AF.Identity, scale=1.0 / 2048)
            i = AA.activation(out=pst[:, 24:32], in_=pst[:, 8:16], func=AF.Square)
            sac.inc(i)
            VV.wait_ge(sac.h, sac.n)
            i = VV.scalar_tensor_tensor(out=pst[:, 32:40], in0=pst[:, 24:32], scalar=-1.0,
                                        in1=pst[:, 16:24], op0=ALU.mult, op1=ALU.add)
            sdv.inc(i)
            AA.wait_ge(sdv.h, sdv.n)
            AA.activation(out=pst[:, 40:48], in_=pst[:, 32:40], func=AF.Ln, bias=epsln[:, 0:1])
            i = AA.activation(out=pst[:, 48:56], in_=pst[:, 40:48], func=AF.Exp, scale=-0.5)
            sac.inc(i)
            VV.wait_ge(sac.h, sac.n)
            for tcc in range(NTC):
                par = tcc % 2
                VV.wait_ge(sout.h, obuf_free[par])
                i = VV.tensor_scalar(out=obuf[par], in0=rf(tcc), scalar1=pst[:, 8 + tcc:9 + tcc],
                                     scalar2=pst[:, 48 + tcc:49 + tcc],
                                     op0=ALU.subtract, op1=ALU.mult)
                sdv.inc(i)
                YY.wait_ge(sdv.h, sdv.n)
                i = YY.dma_start(out=y_out[tcc * 128:(tcc + 1) * 128, :], in_=obuf[par])
                sout.inc(i, 16)
                obuf_free[par] = sout.n
            YY.wait_ge(sout.h, sout.n)
            GG.wait_ge(sout.h, sout.n)
            _scope(None)

        block.sync(body)

    return nc


def kernel(L0_attn_hidden_states, L0_attn_wq, L0_attn_bq, L0_attn_wk, L0_attn_bk,
           L0_attn_wv, L0_attn_bv, L0_attn_wo, L0_attn_bo, L0_attn_wmap_q,
           L0_attn_wmap_k, L0_ffn_gate_up_packed, L0_ffn_gate_up_scales,
           L0_ffn_down_packed, L0_ffn_down_scales):
    x = np.asarray(L0_attn_hidden_states, dtype=np.float32)
    wq = np.asarray(L0_attn_wq, dtype=np.float32)
    wk = np.asarray(L0_attn_wk, dtype=np.float32)
    wv = np.asarray(L0_attn_wv, dtype=np.float32)
    wo = np.asarray(L0_attn_wo, dtype=np.float32)
    bq = np.asarray(L0_attn_bq, dtype=np.float32)
    bk = np.asarray(L0_attn_bk, dtype=np.float32)
    bv = np.asarray(L0_attn_bv, dtype=np.float32)
    bo = np.asarray(L0_attn_bo, dtype=np.float32)
    wmq = np.asarray(L0_attn_wmap_q, dtype=np.float32)
    wmk = np.asarray(L0_attn_wmap_k, dtype=np.float32)
    gup = np.asarray(L0_ffn_gate_up_packed, dtype=np.int32)
    gus = np.asarray(L0_ffn_gate_up_scales, dtype=np.float16)
    dnp = np.asarray(L0_ffn_down_packed, dtype=np.int32)
    dns = np.asarray(L0_ffn_down_scales, dtype=np.float16)

    wqT = np.ascontiguousarray(wq.T)
    wkT = np.ascontiguousarray(wk.T)
    wvT = np.ascontiguousarray(wv.T)
    woT = np.ascontiguousarray(wo.T)
    bqT = np.ascontiguousarray(bq.reshape(NH, 128).T)
    bkT = np.ascontiguousarray(bk.reshape(NH, 128).T)
    wmqT = np.ascontiguousarray(wmq.T)
    wmkT = np.ascontiguousarray(wmk.T)
    gupT = np.ascontiguousarray(gup.T)
    dnT = np.ascontiguousarray(dnp.T)

    sgu = []
    for wc in range(2):
        t = np.repeat(gus[8 * wc:8 * wc + 8], 16, axis=0)
        sgu.append(np.ascontiguousarray(np.repeat(t, 2, axis=1)))
    sdn = np.zeros((6, 128, 2 * D), dtype=np.float16)
    for wcp in range(6):
        r0, r1 = 8 * wcp, min(8 * wcp + 8, 44)
        t = np.repeat(dns[r0:r1], 16, axis=0)
        sdn[wcp, :t.shape[0], :] = np.repeat(t, 2, axis=1)

    ident = np.eye(128, dtype=np.float16)
    onesr = np.ones((1, 128), dtype=np.float16)

    in_maps = []
    for c in range(8):
        b, hf = c // 2, c % 2
        xs = np.ascontiguousarray(x[b, hf * TOK:(hf + 1) * TOK, :])
        m4 = np.zeros((64, 4), dtype=np.float32)
        m4[:, b] = 1.0
        in_maps.append(dict(
            x_in=xs, wqT=wqT, wkT=wkT, wvT=wvT, woT=woT,
            bqT=bqT, bkT=bkT, bv=bv.reshape(1, D), bo=bo.reshape(1, D),
            wmqT=wmqT, wmkT=wmkT, gupT=gupT, dnT=dnT,
            sgu0=sgu[0], sgu1=sgu[1], sdn=sdn,
            ident=ident, onesr=onesr, mask4p=m4,
        ))
    if _RET_MAPS[0]:
        return in_maps
    nc = build()
    res = run_bass_kernel_spmd(nc, in_maps, core_ids=list(range(8)))
    out = np.empty((B, S, D), dtype=np.float32)
    for c in range(8):
        b, hf = c // 2, c % 2
        out[b, hf * TOK:(hf + 1) * TOK, :] = res.results[c]["y"]
    return out


# version 20
# speedup vs baseline: 1.3061x; 1.3061x over previous
import numpy as np
from contextlib import ExitStack

import concourse.bass as bass
import concourse.mybir as mybir
from concourse.bass_utils import run_bass_kernel_spmd

F32 = mybir.dt.float32
F16 = mybir.dt.float16
I32 = mybir.dt.int32
I16 = mybir.dt.int16
I8 = mybir.dt.int8
AF = mybir.ActivationFunctionType
ALU = mybir.AluOpType

B, S, D = 4, 2048, 2048
TOK = 1024
NH, DH, DM = 16, 128, 64
IF = 5632
GUO = 2 * IF
WGU = 256
WDN = 704
NCH = 16
NTC = 8
KB = 1024

DEBUG = False
_RET_MAPS = [False]


class Sem:
    __slots__ = ("h", "n")

    def __init__(self, stk, nc, name):
        self.h = stk.enter_context(nc.semaphore(name))
        self.n = 0

    def inc(self, instr, k=1):
        instr.then_inc(self.h, k)
        self.n += k
        return self.n


def build():
    nc = bass.Bass("TRN2", target_bir_lowering=False)

    x_in = nc.dram_tensor("x_in", [TOK, D], F32, kind="ExternalInput").ap()
    wqT_d = nc.dram_tensor("wqT", [D, D], F32, kind="ExternalInput").ap()
    wkT_d = nc.dram_tensor("wkT", [D, D], F32, kind="ExternalInput").ap()
    wvT_d = nc.dram_tensor("wvT", [D, D], F32, kind="ExternalInput").ap()
    woT_d = nc.dram_tensor("woT", [D, D], F32, kind="ExternalInput").ap()
    bqT_d = nc.dram_tensor("bqT", [128, NH], F32, kind="ExternalInput").ap()
    bkT_d = nc.dram_tensor("bkT", [128, NH], F32, kind="ExternalInput").ap()
    bv_d = nc.dram_tensor("bv", [1, D], F32, kind="ExternalInput").ap()
    bo_d = nc.dram_tensor("bo", [1, D], F32, kind="ExternalInput").ap()
    wmqT_d = nc.dram_tensor("wmqT", [128, DM], F32, kind="ExternalInput").ap()
    wmkT_d = nc.dram_tensor("wmkT", [128, DM], F32, kind="ExternalInput").ap()
    gupT_d = nc.dram_tensor("gupT", [WGU, GUO], I32, kind="ExternalInput").ap()
    dnT_d = nc.dram_tensor("dnT", [WDN, D], I32, kind="ExternalInput").ap()
    sgu_d = [nc.dram_tensor("sgu%d" % w, [128, 2 * GUO], F16, kind="ExternalInput").ap()
             for w in range(2)]
    sdn_d = nc.dram_tensor("sdn", [6, 128, 2 * D], F16, kind="ExternalInput").ap()
    ident_d = nc.dram_tensor("ident", [128, 128], F16, kind="ExternalInput").ap()
    ones_d = nc.dram_tensor("onesr", [1, 128], F16, kind="ExternalInput").ap()
    mask_d = nc.dram_tensor("mask4p", [64, 4], F32, kind="ExternalInput").ap()

    y_out = nc.dram_tensor("y", [TOK, D], F32, kind="ExternalOutput").ap()

    kscr = "ExternalOutput" if DEBUG else "Internal"
    x1_dram = nc.dram_tensor("x1_dram", [TOK, D], F32, kind=kscr).ap()
    wd_dec = nc.dram_tensor("wd_dec", [24 * 128, 2 * D], F16, kind=kscr).ap()
    hT_dram = nc.dram_tensor("hT_dram", [48 * 128, TOK], F16, kind=kscr).ap()
    cc_in = nc.dram_tensor("cc_in", [64, 4 * NH * 129], F32, kind="Internal").ap()
    cc_out = nc.dram_tensor("cc_out", [64, 4 * NH * 129], F32, kind="Internal",
                            addr_space="Shared").ap()
    if DEBUG:
        dbg_kvfull = nc.dram_tensor("dbg_kvfull", [64, NH * 129], F32, kind="ExternalOutput").ap()
        dbg_ln = nc.dram_tensor("dbg_ln", [128, 32], F32, kind="ExternalOutput").ap()
        dbg_rbuf = nc.dram_tensor("dbg_rbuf", [128, D], F32, kind="ExternalOutput").ap()
        dbg_y2 = nc.dram_tensor("dbg_y2", [128, NTC * D], F32, kind="ExternalOutput").ap()
        dbg_kv = nc.dram_tensor("dbg_kv", [64, NH * 129], F32, kind="ExternalOutput").ap()
        dbg_att = nc.dram_tensor("dbg_att", [128, NH * TOK], F16, kind="ExternalOutput").ap()
        dbg_km = nc.dram_tensor("dbg_km", [128, NH * 512], F16, kind="ExternalOutput").ap()
        dbg_qm = nc.dram_tensor("dbg_qm", [64, NH * TOK], F16, kind="ExternalOutput").ap()

    TT, VV, AA, GG, YY = nc.tensor, nc.vector, nc.scalar, nc.gpsimd, nc.sync

    with ExitStack() as top:
        E = top.enter_context
        ident = E(nc.sbuf_tensor("identt", [128, 128], F16))
        onesr = E(nc.sbuf_tensor("onest", [1, 128], F16))
        onescol = E(nc.sbuf_tensor("onescol", [128, 1], F16))
        bqT = E(nc.sbuf_tensor("bqTt", [128, NH], F32))
        bkT = E(nc.sbuf_tensor("bkTt", [128, NH], F32))
        bv16 = E(nc.sbuf_tensor("bv16", [1, D], F16))
        bo16 = E(nc.sbuf_tensor("bo16", [1, D], F16))
        wmq16 = E(nc.sbuf_tensor("wmq16", [128, DM], F16))
        wmk16 = E(nc.sbuf_tensor("wmk16", [128, DM], F16))
        mask4p = E(nc.sbuf_tensor("mask4pt", [64, 4], F32))
        epsatt = E(nc.sbuf_tensor("epsatt", [1, 1], F32))
        EPSATT = 1e-6 * (2.0 ** -12)
        nc.const_aps.register(F32, EPSATT, epsatt[0:1, 0:1])
        epsln = E(nc.sbuf_tensor("epsln", [128, 1], F32))
        epsrms = E(nc.sbuf_tensor("epsrms", [128, 1], F32))
        stats = E(nc.sbuf_tensor("stats", [128, 24], F32))
        mv = E(nc.sbuf_tensor("mv", [128, 2], F32))
        mtmp = E(nc.sbuf_tensor("mtmp", [128, 4], F32))
        flushd = E(nc.sbuf_tensor("flushd", [128, 1], F32))
        pst = E(nc.sbuf_tensor("pst", [128, 64], F32))
        cln16 = E(nc.sbuf_tensor("cln16", [128, 1], F32))
        LN16 = float(-4.0 * np.log(2.0))
        nc.const_aps.register(F32, LN16, cln16[:, 0:1])
        arena = E(nc.sbuf_tensor("arena", [128, 192 * KB], I8))

        def av(off, dt, cols, parts=128):
            b = mybir.dt.size(dt) * cols
            return arena[0:parts, off:off + b].bitcast(dt)

        xT = av(0, F16, NCH * TOK)
        dnpk = av(32 * KB, I32, D)
        dnsc = av(40 * KB, F16, 2 * D)
        dnm = av(48 * KB, I16, 2 * D)
        dnsg = av(56 * KB, I16, 2 * D)
        dno = av(64 * KB, F16, 2 * D)
        elutmp = av(72 * KB, F32, KB)
        xh = [av(76 * KB, F16, D), av(80 * KB, F16, D)]
        kqT = [av(84 * KB, F16, 4 * TOK), av(92 * KB, F16, 4 * TOK)]
        km16 = av(100 * KB, F16, NH * 512)
        vgrp = av(116 * KB, F16, NTC * 512)
        ccpart = av(124 * KB, F32, NH * 129, 64)
        qm16 = av(100 * KB, F16, NH * TOK, 64)
        wsl = [av(133 * KB, F16, NCH * 512), av(149 * KB, F16, NCH * 512)]
        cctmp = av(164 * KB + 512, F32, NH * 129, 64)
        kvfull = av(173 * KB, F32, NH * 129, 64)
        kv16 = av(181 * KB + 512, F16, NH * 128, 64)
        ksum16 = av(185 * KB + 512, F16, NH, 64)
        rn = av(164 * KB + 512, F32, 2 * TOK, 1)
        rn16 = [av(186 * KB, F16, 2 * TOK, 1),
                av(96 * KB, F16, 2 * TOK, 1)]
        wofull = av(32 * KB, F16, NCH * D)
        attT = av(133 * KB, F16, NH * TOK)
        xrel = [av(100 * KB, F32, D), av(108 * KB, F32, D)]
        rbuf = av(116 * KB, F32, D)
        x1f = av(124 * KB, F32, D)
        x1h = av(96 * KB, F16, D)
        rstd4 = mtmp[:, 3:4]
        gpk = [av(165 * KB, I32, KB), av(169 * KB, I32, KB)]
        gsc = [av(173 * KB, F16, 2 * KB), av(177 * KB, F16, 2 * KB)]
        gbuf = av(32 * KB, F16, 8 * 2048)
        ubuf = av(64 * KB, F16, 8 * 2048)
        gsil = av(96 * KB, F16, 8 * KB)
        gtm = av(112 * KB, I16, 2 * KB)
        gts = av(116 * KB, I16, 2 * KB)
        up16 = [av((120 + j) * KB, F16, 512) for j in range(4)]
        h16 = [av(124 * KB, F16, 512), av(125 * KB, F16, 512)]
        y2 = av(32 * KB, F32, NTC * D)
        hts = [av(96 * KB, F16, 8 * KB), av(112 * KB, F16, 8 * KB)]
        wds = [av(128 * KB, F16, 4 * KB), av(136 * KB, F16, 4 * KB)]
        r2full = av(96 * KB, F32, NTC * D)
        x1r = [av(160 * KB, F32, D), av(168 * KB, F32, D)]
        obuf = [av(176 * KB, F32, D), av(184 * KB, F32, D)]

        pb = [E(nc.psum_tensor("pb%d" % i, [128, 512], F32)) for i in range(8)]

        sx = Sem(top, nc, "sx")
        sg = Sem(top, nc, "sg")
        sw = [Sem(top, nc, "sw%d" % i) for i in range(2)]
        spk = [Sem(top, nc, "spk%d" % i) for i in range(2)]
        sdnl = Sem(top, nc, "sdnl")
        spe = Sem(top, nc, "spe")
        sdv = Sem(top, nc, "sdv")
        sac = Sem(top, nc, "sac")
        scc = Sem(top, nc, "scc")
        scin = Sem(top, nc, "scin")
        swd = Sem(top, nc, "swd")
        sht = Sem(top, nc, "sht")
        sx1 = Sem(top, nc, "sx1")
        sout = Sem(top, nc, "sout")
        sxr = [Sem(top, nc, "sxr%d" % i) for i in range(2)]
        shr = [Sem(top, nc, "shr%d" % i) for i in range(2)]
        swr = [Sem(top, nc, "swr%d" % i) for i in range(2)]

        bankW = [0] * 8
        bankR = [None] * 8

        def pe_acquire(i):
            if bankR[i] is not None:
                s, v = bankR[i]
                TT.wait_ge(s.h, v)
                bankR[i] = None

        def bank_done(i):
            bankW[i] = spe.n

        def rd_wait(eng, i):
            eng.wait_ge(spe.h, bankW[i])

        block = E(nc.Block())
        _scope_state = [None]

        def _scope(name):
            if _scope_state[0] is not None:
                n, sid = _scope_state[0]
                nc.leave_named_scope(n, sid, False)
                _scope_state[0] = None
            if name is not None:
                sid = nc.enter_named_scope(name, False)[0]
                _scope_state[0] = (name, sid)

        def body(_):
            YY.dma_start(out=ident[:], in_=ident_d).then_inc(sx.h, 16); sx.n += 16
            YY.dma_start(out=onesr[:], in_=ones_d).then_inc(sx.h, 16); sx.n += 16
            YY.dma_start(out=bqT[:], in_=bqT_d).then_inc(sx.h, 16); sx.n += 16
            YY.dma_start(out=bkT[:], in_=bkT_d).then_inc(sx.h, 16); sx.n += 16
            YY.dma_start(out=mask4p[:], in_=mask_d).then_inc(sx.h, 16); sx.n += 16
            consts_x = sx.n
            GG.dma_start(out=bv16[:], in_=bv_d).then_inc(sg.h, 16); sg.n += 16
            GG.dma_start(out=bo16[:], in_=bo_d).then_inc(sg.h, 16); sg.n += 16
            GG.dma_start(out=wmq16[:], in_=wmqT_d).then_inc(sg.h, 16); sg.n += 16
            GG.dma_start(out=wmk16[:], in_=wmkT_d).then_inc(sg.h, 16); sg.n += 16
            consts_g = sg.n
            VV.memset(onescol[:], 1.0)
            VV.memset(epsatt[:], 1e-6 * (2.0 ** -12))
            VV.memset(epsln[:], 1e-5)
            VV.memset(epsrms[:], 1e-6 * (2.0 ** -8))
            i = VV.memset(cln16[:], LN16)
            sdv.inc(i)

            dnstate = {"wcp": 0, "kp": 0, "last_dv": 0, "pk_rdy": 0}

            def dn_unit():
                wcp, kp = dnstate["wcp"], dnstate["kp"]
                if wcp >= 6:
                    return
                rows = 64 if wcp == 5 else 128
                if kp == 0:
                    YY.wait_ge(sdv.h, dnstate["last_dv"])
                    i = YY.dma_start(out=av(32 * KB, I32, D, rows),
                                     in_=dnT_d[wcp * 128:wcp * 128 + rows, :])
                    sdnl.inc(i, 16)
                    i = YY.dma_start(out=av(40 * KB, F16, 2 * D, rows), in_=sdn_d[wcp, 0:rows, :])
                    sdnl.inc(i, 16)
                    VV.wait_ge(sdnl.h, sdnl.n)
                    VV.tensor_scalar(out=av(40 * KB, F16, 2 * D, rows),
                                     in0=av(40 * KB, F16, 2 * D, rows),
                                     scalar1=float(2.0 ** 14), scalar2=None, op0=ALU.mult)
                    dnstate["pk_rdy"] = sdnl.n
                rows_ = rows
                pk16 = av(32 * KB, I32, D, rows_).bitcast(I16)
                tm = av(48 * KB, I16, 2 * D, rows_)
                tsg = av(56 * KB, I16, 2 * D, rows_)
                to = av(64 * KB, F16, 2 * D, rows_)
                sc = av(40 * KB, F16, 2 * D, rows_)
                sm = 9 - 4 * kp
                if sm >= 0:
                    VV.tensor_scalar(out=tm, in0=pk16, scalar1=sm, scalar2=0x0E00,
                                     op0=ALU.logical_shift_left, op1=ALU.bitwise_and)
                else:
                    VV.tensor_scalar(out=tm, in0=pk16, scalar1=-sm, scalar2=0x0E00,
                                     op0=ALU.logical_shift_right, op1=ALU.bitwise_and)
                ss = 12 - 4 * kp
                if ss > 0:
                    VV.tensor_scalar(out=tsg, in0=pk16, scalar1=ss, scalar2=-32768,
                                     op0=ALU.logical_shift_left, op1=ALU.bitwise_and)
                else:
                    VV.tensor_scalar(out=tsg, in0=pk16, scalar1=-32768, scalar2=None,
                                     op0=ALU.bitwise_and)
                VV.tensor_tensor(out=tm, in0=tm, in1=tsg, op=ALU.bitwise_or)
                VV.wait_ge(swd.h, swd.n)
                i = VV.tensor_tensor(out=to, in0=tm.bitcast(F16), in1=sc, op=ALU.mult)
                sdv.inc(i)
                dnstate["last_dv"] = sdv.n
                YY.wait_ge(sdv.h, sdv.n)
                r0 = (wcp * 4 + kp) * 128
                i = YY.dma_start(out=wd_dec[r0:r0 + rows_, :], in_=to)
                swd.inc(i, 16)
                if kp == 3:
                    dnstate["wcp"] += 1
                    dnstate["kp"] = 0
                else:
                    dnstate["kp"] = kp + 1

            _scope("P1_xT")
            xT3 = xT.rearrange("p (c t) -> p c t", c=NCH)
            xh_free = [0, 0]
            xh_rdy = [0, 0]
            for tc in range(NTC):
                par = tc % 2
                GG.wait_ge(spe.h, xh_free[par])
                i = GG.dma_start(out=xh[par], in_=x_in[tc * 128:(tc + 1) * 128, :])
                sg.inc(i, 16)
                xh_rdy[par] = sg.n
                TT.wait_ge(sg.h, xh_rdy[par])
                if tc == 0:
                    TT.wait_ge(sx.h, 16)
                for half in range(2):
                    bk = (2 * tc) % 4 + half
                    pe_acquire(bk)
                    pv = pb[bk][:].bitcast(F16)
                    for j in range(8):
                        ic = half * 8 + j
                        i = TT.transpose(pv[:, j * 128:(j + 1) * 128],
                                         xh[par][:, ic * 128:(ic + 1) * 128], ident[:])
                    spe.inc(i)
                    bank_done(bk)
                    eng, seng = (VV, sdv) if half == 0 else (AA, sac)
                    rd_wait(eng, bk)
                    dst = xT3[:, half * 8:(half + 1) * 8, tc * 128:(tc + 1) * 128]
                    src = pv[:, 0:1024].rearrange("p (c t) -> p c t", c=8)
                    if half == 0:
                        i = VV.tensor_copy(out=dst, in_=src)
                    else:
                        i = AA.copy(out=dst, in_=src)
                    seng.inc(i)
                    bankR[bk] = (seng, seng.n)
                xh_free[par] = spe.n
                dn_unit()
            xT_dv, xT_ac = sdv.n, sac.n

            slab_free = [0, 0]
            slab_rdy = [0, 0]
            kq_free = [0, 0]

            def emit_proj(wT_d, kind, biasT=None):
                first_call_wait = (kind == "k")
                for sb in range(4):
                    par = sb % 2
                    GG.wait_ge(spe.h, slab_free[par])
                    src = wT_d.rearrange("(c p) o -> p c o", p=128)[:, :, sb * 512:(sb + 1) * 512]
                    dstw = wsl[par].rearrange("p (c o) -> p c o", c=NCH)
                    i = GG.dma_start(out=dstw, in_=src)
                    sw[par].inc(i, 16)
                    slab_rdy[par] = sw[par].n

                    TT.wait_ge(sw[par].h, slab_rdy[par])
                    if sb == 0 and first_call_wait:
                        TT.wait_ge(sdv.h, xT_dv)
                        TT.wait_ge(sac.h, xT_ac)
                    if kind in ("k", "q"):
                        for ic in range(NCH):
                            for h4 in range(4):
                                for th2 in range(2):
                                    bk = h4 * 2 + th2
                                    if ic == 0:
                                        pe_acquire(bk)
                                    i = TT.matmul(
                                        pb[bk][:],
                                        wsl[par][:, ic * 512 + h4 * 128: ic * 512 + h4 * 128 + 128],
                                        xT[:, ic * TOK + th2 * 512: ic * TOK + th2 * 512 + 512],
                                        start=(ic == 0), stop=(ic == NCH - 1))
                        spe.inc(i)
                        for bk in range(8):
                            bank_done(bk)
                        slab_free[par] = spe.n
                        kqb = kqT[par]
                        VV.wait_ge(spe.h, spe.n)
                        VV.wait_ge(spe.h, kq_free[par])
                        if sb == 0:
                            VV.wait_ge(sx.h, consts_x)
                        for h4 in range(4):
                            for th2 in range(2):
                                bk = h4 * 2 + th2
                                i = VV.tensor_scalar(
                                    out=kqb[:, h4 * TOK + th2 * 512: h4 * TOK + th2 * 512 + 512],
                                    in0=pb[bk][:], scalar1=biasT[:, sb * 4 + h4: sb * 4 + h4 + 1],
                                    scalar2=None, op0=ALU.add)
                                sdv.inc(i)
                                bankR[bk] = (sdv, sdv.n)
                        evac_dv = sdv.n
                        wmap = wmk16 if kind == "k" else wmq16
                        TT.wait_ge(sdv.h, evac_dv)
                        if sb == 0 and kind == "k":
                            TT.wait_ge(sg.h, consts_g)
                        for h4 in range(4):
                            h = sb * 4 + h4
                            if kind == "k":
                                bk = h4 * 2
                                pe_acquire(bk)
                                for tcc in range(NTC):
                                    i = TT.matmul(
                                        pb[bk][:, tcc * 64:(tcc + 1) * 64],
                                        kqb[:, h4 * TOK + tcc * 128: h4 * TOK + tcc * 128 + 128],
                                        wmap[:], start=(tcc == 0), stop=(tcc == NTC - 1),
                                        skip_group_check=True)
                                spe.inc(i)
                                bank_done(bk)
                                rd_wait(VV, bk)
                                i = VV.tensor_scalar(out=elutmp[:, 0:512], in0=pb[bk][:],
                                                     scalar1=0.0, scalar2=None, op0=ALU.min)
                                sdv.inc(i)
                                rd_wait(AA, bk)
                                AA.wait_ge(sdv.h, sdv.n)
                                i = AA.activation(out=elutmp[:, 512:1024], in_=pb[bk][:], func=AF.Relu)
                                sac.inc(i)
                                bankR[bk] = (sac, sac.n)
                                i = AA.activation(out=elutmp[:, 0:512], in_=elutmp[:, 0:512], func=AF.Exp)
                                sac.inc(i)
                                VV.wait_ge(sac.h, sac.n)
                                i = VV.tensor_tensor(out=km16[:, h * 512:(h + 1) * 512],
                                                     in0=elutmp[:, 0:512], in1=elutmp[:, 512:1024],
                                                     op=ALU.add)
                                sdv.inc(i)
                            else:
                                for th2 in range(2):
                                    bk = h4 * 2 + th2
                                    pe_acquire(bk)
                                    i = TT.matmul(
                                        pb[bk][0:64, :], wmap[:],
                                        kqb[:, h4 * TOK + th2 * 512: h4 * TOK + th2 * 512 + 512],
                                        start=True, stop=True)
                                    spe.inc(i)
                                    bank_done(bk)
                                    rd_wait(VV, bk)
                                    i = VV.tensor_scalar(out=elutmp[0:64, 0:512], in0=pb[bk][0:64, :],
                                                         scalar1=0.0, scalar2=None, op0=ALU.min)
                                    sdv.inc(i)
                                    rd_wait(AA, bk)
                                    AA.wait_ge(sdv.h, sdv.n)
                                    i = AA.activation(out=elutmp[0:64, 512:1024], in_=pb[bk][0:64, :],
                                                      func=AF.Relu)
                                    sac.inc(i)
                                    bankR[bk] = (sac, sac.n)
                                    i = AA.activation(out=elutmp[0:64, 0:512], in_=elutmp[0:64, 0:512],
                                                      func=AF.Exp)
                                    sac.inc(i)
                                    VV.wait_ge(sac.h, sac.n)
                                    i = VV.tensor_tensor(
                                        out=qm16[:, h * TOK + th2 * 512: h * TOK + th2 * 512 + 512],
                                        in0=elutmp[0:64, 0:512], in1=elutmp[0:64, 512:1024],
                                        op=ALU.add)
                                    sdv.inc(i)
                        kq_free[par] = spe.n
                        dn_unit()
                    else:
                        if sb == 0:
                            TT.wait_ge(sx.h, consts_x)
                            TT.wait_ge(sg.h, consts_g)
                            TT.wait_ge(sdv.h, sdv.n)
                        pe_acquire(4)
                        pe_acquire(5)
                        for tcg in range(2):
                            for ic in range(NCH):
                                for t4 in range(4):
                                    tcc = tcg * 4 + t4
                                    bk = t4
                                    if ic == 0:
                                        pe_acquire(bk)
                                    i = TT.matmul(
                                        pb[bk][:],
                                        xT[:, ic * TOK + tcc * 128: ic * TOK + tcc * 128 + 128],
                                        wsl[par][:, ic * 512: ic * 512 + 512],
                                        start=(ic == 0), stop=False)
                            for t4 in range(4):
                                i = TT.matmul(pb[t4][:], onesr[0:1, 0:128],
                                              bv16[0:1, sb * 512:(sb + 1) * 512],
                                              start=False, stop=True)
                            spe.inc(i)
                            for t4 in range(4):
                                bank_done(t4)
                            AA.wait_ge(spe.h, spe.n)
                            for t4 in range(4):
                                tcc = tcg * 4 + t4
                                i = AA.copy(out=vgrp[:, tcc * 512:(tcc + 1) * 512], in_=pb[t4][:])
                                sac.inc(i)
                                bankR[t4] = (sac, sac.n)
                            TT.wait_ge(sac.h, sac.n)
                            for h4 in range(4):
                                h = sb * 4 + h4
                                for t4 in range(4):
                                    tcc = tcg * 4 + t4
                                    st = (tcg == 0 and h4 == 0 and t4 == 0)
                                    sp = (tcg == 1 and h4 == 3 and t4 == 3)
                                    i = TT.matmul(
                                        pb[4][0:64, h4 * 128:(h4 + 1) * 128],
                                        km16[:, h * 512 + tcc * 64: h * 512 + tcc * 64 + 64],
                                        vgrp[:, tcc * 512 + h4 * 128: tcc * 512 + h4 * 128 + 128],
                                        start=st, stop=sp, skip_group_check=True)
                                    i = TT.matmul(
                                        pb[5][0:64, h:h + 1],
                                        km16[:, h * 512 + tcc * 64: h * 512 + tcc * 64 + 64],
                                        onescol[:], start=st, stop=sp, skip_group_check=True)
                        spe.inc(i)
                        bank_done(4)
                        bank_done(5)
                        slab_free[par] = spe.n
                        VV.wait_ge(spe.h, spe.n)
                        for h4 in range(4):
                            h = sb * 4 + h4
                            i = VV.tensor_copy(out=ccpart[:, h * 129: h * 129 + 128],
                                               in_=pb[4][0:64, h4 * 128:(h4 + 1) * 128])
                            sdv.inc(i)
                            i = VV.tensor_copy(out=ccpart[:, h * 129 + 128: h * 129 + 129],
                                               in_=pb[5][0:64, h:h + 1])
                            sdv.inc(i)
                        bankR[4] = (sdv, sdv.n)
                        bankR[5] = (sdv, sdv.n)
                        dn_unit()
                        dn_unit()

            _scope("P2_K")
            emit_proj(wkT_d, "k", biasT=bkT)
            if DEBUG:
                YY.wait_ge(sdv.h, sdv.n)
                i = YY.dma_start(out=dbg_km, in_=km16); sout.inc(i, 16)
            _scope("P3_V")
            emit_proj(wvT_d, "v")
            while dnstate["wcp"] < 6:
                dn_unit()
            _scope("P4_cc")
            for b in range(4):
                i = VV.tensor_scalar(out=cctmp, in0=ccpart, scalar1=mask4p[0:64, b:b + 1],
                                     scalar2=None, op0=ALU.mult)
                sdv.inc(i)
                YY.wait_ge(sdv.h, sdv.n)
                i = YY.dma_start(out=cc_in[:, b * NH * 129:(b + 1) * NH * 129], in_=cctmp)
                scin.inc(i, 16)
                if b < 3:
                    VV.wait_ge(scin.h, scin.n)
            GG.wait_ge(scin.h, scin.n)
            i = GG.collective_compute("AllReduce", ALU.add,
                                      replica_groups=[[0, 1, 2, 3, 4, 5, 6, 7]],
                                      ins=[cc_in.opt()], outs=[cc_out.opt()])
            i.then_inc(scc.h)
            scc.n = 1
            if DEBUG:
                YY.wait_ge(sdv.h, sdv.n)
                i = YY.dma_start(out=dbg_kv, in_=ccpart); sout.inc(i, 16)

            _scope("P5_Q")
            emit_proj(wqT_d, "q", biasT=bqT)
            qm_dv = sdv.n
            if DEBUG:
                YY.wait_ge(sdv.h, qm_dv)
                i = YY.dma_start(out=dbg_qm, in_=qm16); sout.inc(i, 16)
                VV.wait_ge(sout.h, sout.n)

            GG.wait_ge(spe.h, kq_free[0])
            GG.wait_ge(spe.h, kq_free[1])
            GG.wait_ge(sdv.h, qm_dv)
            GG.wait_ge(sac.h, sac.n)
            src = woT_d.rearrange("(c p) o -> p c o", p=128)
            i = GG.dma_start(out=wofull.rearrange("p (c o) -> p c o", c=NCH), in_=src)
            sg.inc(i, 16)
            wo_rdy = sg.n

            _scope("P6_att")
            GG.wait_ge(scc.h, 1)
            for b in range(4):
                GG.wait_ge(sdv.h, sdv.n)
                i = GG.dma_start(out=cctmp, in_=cc_out[:, b * NH * 129:(b + 1) * NH * 129])
                sg.inc(i, 16)
                VV.wait_ge(sg.h, sg.n)
                if b == 0:
                    i = VV.tensor_scalar(out=kvfull, in0=cctmp, scalar1=mask4p[0:64, 0:1],
                                         scalar2=None, op0=ALU.mult)
                else:
                    i = VV.scalar_tensor_tensor(out=kvfull, in0=cctmp,
                                                scalar=mask4p[0:64, b:b + 1], in1=kvfull,
                                                op0=ALU.mult, op1=ALU.add)
                sdv.inc(i)
            for h in range(NH):
                VV.tensor_scalar(out=kv16[:, h * 128:(h + 1) * 128],
                                 in0=kvfull[:, h * 129:h * 129 + 128],
                                 scalar1=float(2.0 ** -12), scalar2=None, op0=ALU.mult)
            i = VV.tensor_copy(out=ksum16, in_=kvfull[:, 128:NH * 129:129])
            sdv.inc(i)
            kv_rdy = sdv.n
            if DEBUG:
                YY.wait_ge(sdv.h, kv_rdy)
                i = YY.dma_start(out=dbg_kvfull, in_=kvfull); sout.inc(i, 16)

            rn_dv = [0] * 8
            qmn_dv = [0] * 8
            TT.wait_ge(sdv.h, kv_rdy)
            NG = 8
            for s in range(NG + 3):
                g = s - 3
                if 0 <= g < NG:
                    h0 = g * 2
                    TT.wait_ge(sdv.h, qmn_dv[g])
                    for h4 in range(2):
                        h = h0 + h4
                        for th2 in range(2):
                            bk = 4 + h4 * 2 + th2
                            pe_acquire(bk)
                            i = TT.matmul(pb[bk][:], kv16[:, h * 128:(h + 1) * 128],
                                          qm16[:, h * TOK + th2 * 512: h * TOK + th2 * 512 + 512],
                                          start=True, stop=True)
                            spe.inc(i)
                            bank_done(bk)
                    AA.wait_ge(spe.h, spe.n)
                    for h4 in range(2):
                        h = h0 + h4
                        for th2 in range(2):
                            bk = 4 + h4 * 2 + th2
                            i = AA.copy(out=attT[:, h * TOK + th2 * 512: h * TOK + th2 * 512 + 512],
                                        in_=pb[bk][:])
                            sac.inc(i)
                            bankR[bk] = (sac, sac.n)
                g = s - 2
                if 0 <= g < NG:
                    p2 = g % 2
                    TT.wait_ge(sdv.h, rn_dv[g])
                    for h4 in range(2):
                        bk = 2 + h4
                        pe_acquire(bk)
                        for th2 in range(2):
                            i = TT.matmul(pb[bk][64 * th2:64 * th2 + 64, :], onesr[0:1, 0:64],
                                          rn16[p2][0:1, (h4 * 2 + th2) * 512:(h4 * 2 + th2) * 512 + 512],
                                          start=True, stop=True, skip_group_check=True)
                        spe.inc(i)
                        bank_done(bk)
                    rd_wait(VV, 2)
                    rd_wait(VV, 3)
                    h0 = g * 2
                    for h4 in range(2):
                        h = h0 + h4
                        for th2 in range(2):
                            i = VV.tensor_tensor(
                                out=qm16[:, h * TOK + th2 * 512: h * TOK + th2 * 512 + 512],
                                in0=qm16[:, h * TOK + th2 * 512: h * TOK + th2 * 512 + 512],
                                in1=pb[2 + h4][64 * th2:64 * th2 + 64, :], op=ALU.mult)
                            sdv.inc(i)
                    bankR[2] = (sdv, sdv.n)
                    bankR[3] = (sdv, sdv.n)
                    qmn_dv[g] = sdv.n
                g = s - 1
                if 0 <= g < NG:
                    p2 = g % 2
                    rd_wait(AA, 0)
                    rd_wait(AA, 1)
                    if g >= 1:
                        AA.wait_ge(sdv.h, rn_dv[g - 1])
                    for h4 in range(2):
                        for th2 in range(2):
                            i = AA.activation(
                                out=rn[0:1, (h4 * 2 + th2) * 512:(h4 * 2 + th2) * 512 + 512],
                                in_=pb[h4][64 * th2:64 * th2 + 1, :],
                                func=AF.Copy, scale=float(2.0 ** -12), bias=EPSATT)
                            sac.inc(i)
                    bankR[0] = (sac, sac.n)
                    bankR[1] = (sac, sac.n)
                    VV.wait_ge(sac.h, sac.n)
                    VV.reciprocal(out=rn[0:1, :], in_=rn[0:1, :])
                    i = VV.tensor_copy(out=rn16[p2][0:1, :], in_=rn[0:1, :])
                    sdv.inc(i)
                    rn_dv[g] = sdv.n
                g = s
                if g < NG:
                    h0 = g * 2
                    for h4 in range(2):
                        h = h0 + h4
                        bk = h4
                        pe_acquire(bk)
                        for th2 in range(2):
                            i = TT.matmul(pb[bk][64 * th2:64 * th2 + 1, :], ksum16[:, h:h + 1],
                                          qm16[:, h * TOK + th2 * 512: h * TOK + th2 * 512 + 512],
                                          start=True, stop=True, skip_group_check=True)
                        spe.inc(i)
                        bank_done(bk)
            att_ac = sac.n
            if DEBUG:
                YY.wait_ge(sac.h, att_ac)
                i = YY.dma_start(out=dbg_att, in_=attT); sout.inc(i, 16)

            _scope("P7_Wo")
            x1T3 = xT.rearrange("p (c t) -> p c t", c=NCH)
            TT.wait_ge(sg.h, wo_rdy)
            TT.wait_ge(sac.h, att_ac)
            YY.wait_ge(spe.h, spe.n)
            xrel_free = [0, 0]
            x1h_pe = 0
            for tc in range(NTC):
                par = tc % 2
                YY.wait_ge(sdv.h, xrel_free[par])
                i = YY.dma_start(out=xrel[par], in_=x_in[tc * 128:(tc + 1) * 128, :])
                sxr[par].inc(i, 16)
                for os in range(4):
                    bk = os
                    pe_acquire(bk)
                    for hh in range(NH):
                        i = TT.matmul(pb[bk][:],
                                      attT[:, hh * TOK + tc * 128: hh * TOK + tc * 128 + 128],
                                      wofull[:, hh * D + os * 512: hh * D + os * 512 + 512],
                                      start=(hh == 0), stop=False)
                    i = TT.matmul(pb[bk][:], onesr[0:1, 0:128],
                                  bo16[0:1, os * 512:(os + 1) * 512], start=False, stop=True)
                    spe.inc(i)
                    bank_done(bk)
                VV.wait_ge(sxr[par].h, sxr[par].n)
                VV.wait_ge(spe.h, spe.n)
                for os in range(4):
                    i = VV.tensor_tensor(out=rbuf[:, os * 512:(os + 1) * 512],
                                         in0=xrel[par][:, os * 512:(os + 1) * 512],
                                         in1=pb[os][:], op=ALU.add)
                    sdv.inc(i)
                    bankR[os] = (sdv, sdv.n)
                xrel_free[par] = sdv.n
                AA.wait_ge(sdv.h, sdv.n)
                AA.wait_ge(sx1.h, sx1.n)
                AA.activation(out=x1f, in_=rbuf, func=AF.Square, accum_out=stats[:, 8:9])
                AA.activation(out=x1f, in_=rbuf, func=AF.Identity, accum_out=stats[:, 9:10])
                AA.activation(out=flushd[:], in_=stats[:, 8:9], func=AF.Copy)
                AA.activation(out=flushd[:], in_=stats[:, 9:10], func=AF.Copy)
                AA.activation(out=mv[:, 0:1], in_=stats[:, 9:10], func=AF.Identity, scale=1.0 / 2048)
                AA.activation(out=mtmp[:, 3:4], in_=stats[:, 8:9], func=AF.Identity, scale=1.0 / 2048)
                AA.activation(out=mtmp[:, 2:3], in_=mv[:, 0:1], func=AF.Square)
                AA.activation(out=mv[:, 1:2], in_=mtmp[:, 2:3], func=AF.Identity,
                              scale=-1.0, bias=mtmp[:, 3:4])
                AA.activation(out=mtmp[:, 0:1], in_=mv[:, 1:2], func=AF.Ln, bias=epsln[:, 0:1])
                AA.activation(out=mtmp[:, 1:2], in_=mtmp[:, 0:1], func=AF.Exp, scale=-0.5)
                i = AA.activation(out=rstd4, in_=mtmp[:, 0:1], func=AF.Exp, scale=-0.5, bias=LN16)
                sac.inc(i)
                VV.wait_ge(sac.h, sac.n)
                VV.wait_ge(sx1.h, sx1.n)
                i = VV.tensor_scalar(out=x1f, in0=rbuf, scalar1=mv[:, 0:1],
                                     scalar2=mtmp[:, 1:2], op0=ALU.subtract, op1=ALU.mult)
                sdv.inc(i)
                YY.wait_ge(sdv.h, sdv.n)
                i = YY.dma_start(out=x1_dram[tc * 128:(tc + 1) * 128, :], in_=x1f)
                sx1.inc(i, 16)
                VV.wait_ge(spe.h, x1h_pe)
                i = VV.tensor_scalar(out=x1h, in0=rbuf, scalar1=mv[:, 0:1],
                                     scalar2=rstd4, op0=ALU.subtract, op1=ALU.mult)
                sdv.inc(i)
                x1h_dv = sdv.n
                TT.wait_ge(sdv.h, x1h_dv)
                for half in range(2):
                    bk = 4 + half
                    pe_acquire(bk)
                    pv = pb[bk][:].bitcast(F16)
                    for j in range(8):
                        kidx = half * 8 + j
                        wc, kk = kidx // 8, kidx % 8
                        col0 = 1024 * wc + kk
                        i = TT.transpose(pv[:, j * 128:(j + 1) * 128],
                                         x1h[:, col0:col0 + 1017:8], ident[:])
                    spe.inc(i)
                    bank_done(bk)
                    eng, seng = (VV, sdv) if half == 0 else (AA, sac)
                    rd_wait(eng, bk)
                    dst = x1T3[:, half * 8:(half + 1) * 8, tc * 128:(tc + 1) * 128]
                    srcp = pv[:, 0:1024].rearrange("p (c t) -> p c t", c=8)
                    if half == 0:
                        i = VV.tensor_copy(out=dst, in_=srcp)
                    else:
                        i = AA.copy(out=dst, in_=srcp)
                    seng.inc(i)
                    bankR[bk] = (seng, seng.n)
                x1h_pe = spe.n
                if DEBUG and tc == 7:
                    YY.wait_ge(sdv.h, sdv.n)
                    i = YY.dma_start(out=dbg_ln[:, 0:24], in_=stats[:]); sout.inc(i, 16)
                    i = YY.dma_start(out=dbg_ln[:, 24:26], in_=mv[:]); sout.inc(i, 16)
                    i = YY.dma_start(out=dbg_ln[:, 26:30], in_=mtmp[:]); sout.inc(i, 16)
                    i = YY.dma_start(out=dbg_rbuf, in_=rbuf); sout.inc(i, 16)
            x1T_dv, x1T_ac = sdv.n, sac.n

            _scope("P8_gu")
            ldp = [0]
            gpk_free = [0, 0]

            def decode_range(dstbuf, obase, owidth):
                for wc in range(2):
                    par = ldp[0] % 2
                    ldp[0] += 1
                    YY.wait_ge(sdv.h, gpk_free[par])
                    i = YY.dma_start(out=gpk[par][:, 0:owidth],
                                     in_=gupT_d[wc * 128:(wc + 1) * 128, obase:obase + owidth])
                    spk[par].inc(i, 16)
                    i = YY.dma_start(out=gsc[par][:, 0:2 * owidth],
                                     in_=sgu_d[wc][:, 2 * obase:2 * obase + 2 * owidth])
                    spk[par].inc(i, 16)
                    VV.wait_ge(spk[par].h, spk[par].n)
                    VV.tensor_scalar(out=gsc[par][:, 0:2 * owidth], in0=gsc[par][:, 0:2 * owidth],
                                     scalar1=float(2.0 ** 14), scalar2=None, op0=ALU.mult)
                    pk16 = gpk[par][:, 0:owidth].bitcast(I16)
                    for kp in range(4):
                        sm = 9 - 4 * kp
                        if sm >= 0:
                            VV.tensor_scalar(out=gtm[:, 0:2 * owidth], in0=pk16, scalar1=sm,
                                             scalar2=0x0E00, op0=ALU.logical_shift_left,
                                             op1=ALU.bitwise_and)
                        else:
                            VV.tensor_scalar(out=gtm[:, 0:2 * owidth], in0=pk16, scalar1=-sm,
                                             scalar2=0x0E00, op0=ALU.logical_shift_right,
                                             op1=ALU.bitwise_and)
                        ss = 12 - 4 * kp
                        if ss > 0:
                            VV.tensor_scalar(out=gts[:, 0:2 * owidth], in0=pk16, scalar1=ss,
                                             scalar2=-32768, op0=ALU.logical_shift_left,
                                             op1=ALU.bitwise_and)
                        else:
                            VV.tensor_scalar(out=gts[:, 0:2 * owidth], in0=pk16, scalar1=-32768,
                                             scalar2=None, op0=ALU.bitwise_and)
                        VV.tensor_tensor(out=gtm[:, 0:2 * owidth], in0=gtm[:, 0:2 * owidth],
                                         in1=gts[:, 0:2 * owidth], op=ALU.bitwise_or)
                        ti = wc * 4 + kp
                        i = VV.tensor_tensor(out=dstbuf[:, ti * 2048: ti * 2048 + 2 * owidth],
                                             in0=gtm[:, 0:2 * owidth].bitcast(F16),
                                             in1=gsc[par][:, 0:2 * owidth], op=ALU.mult)
                        sdv.inc(i)
                    gpk_free[par] = sdv.n
                return sdv.n

            gbuf_pe = 0
            ubuf_pe = 0
            h16_free = [0, 0]
            hm_dv = [0] * 16
            up16_free = [0] * 4
            OWS = [1024, 1024, 1024, 1024, 1024, 512]
            g_rdy = [0] * 6
            u_rdy = [0] * 6
            g_rdy[0] = decode_range(gbuf, 0, OWS[0])
            u_rdy[0] = decode_range(ubuf, IF, OWS[0])
            for jc in range(6):
                ow = OWS[jc]
                M = 64 if jc == 5 else 128
                TT.wait_ge(sdv.h, g_rdy[jc])
                if jc == 0:
                    TT.wait_ge(sdv.h, x1T_dv)
                    TT.wait_ge(sac.h, x1T_ac)
                for kq in range(8):
                    for th in range(2):
                        bk = (kq * 2 + th) % 4
                        pe_acquire(bk)
                        first = True
                        for wc in range(2):
                            for kp in range(4):
                                for ln in range(2):
                                    ti = wc * 4 + kp
                                    cix = wc * 8 + kp + 4 * ln
                                    st0 = ti * 2048 + 2 * kq + ln
                                    i = TT.matmul(
                                        pb[bk][0:M, :],
                                        gbuf[:, st0: st0 + 16 * (M - 1) + 1: 16],
                                        xT[:, cix * TOK + th * 512: cix * TOK + th * 512 + 512],
                                        start=first, stop=(wc == 1 and kp == 3 and ln == 1))
                                    first = False
                        spe.inc(i)
                        bank_done(bk)
                        rd_wait(AA, bk)
                        AA.wait_ge(sdv.h, hm_dv[kq * 2 + th])
                        i = AA.activation(
                            out=gsil[0:M, kq * 1024 + th * 512: kq * 1024 + th * 512 + 512],
                            in_=pb[bk][0:M, :], func=AF.Silu, scale=16.0)
                        sac.inc(i)
                        bankR[bk] = (sac, sac.n)
                gbuf_pe = spe.n
                if jc < 5:
                    VV.wait_ge(spe.h, gbuf_pe)
                    g_rdy[jc + 1] = decode_range(gbuf, (jc + 1) * 1024, OWS[jc + 1])
                TT.wait_ge(sdv.h, u_rdy[jc])
                for kq in range(8):
                    ci = jc * 8 + kq
                    for th in range(2):
                        bk = 4 + (kq * 2 + th) % 4
                        pe_acquire(bk)
                        first = True
                        for wc in range(2):
                            for kp in range(4):
                                for ln in range(2):
                                    ti = wc * 4 + kp
                                    cix = wc * 8 + kp + 4 * ln
                                    st0 = ti * 2048 + 2 * kq + ln
                                    i = TT.matmul(
                                        pb[bk][0:M, :],
                                        ubuf[:, st0: st0 + 16 * (M - 1) + 1: 16],
                                        xT[:, cix * TOK + th * 512: cix * TOK + th * 512 + 512],
                                        start=first, stop=(wc == 1 and kp == 3 and ln == 1))
                                    first = False
                        spe.inc(i)
                        bank_done(bk)
                        rd_wait(AA, bk)
                        sl4 = (kq * 2 + th) % 4
                        AA.wait_ge(sdv.h, up16_free[sl4])
                        i = AA.copy(out=up16[sl4][0:M, :], in_=pb[bk][0:M, :])
                        sac.inc(i)
                        bankR[bk] = (sac, sac.n)
                        hp = th
                        VV.wait_ge(sac.h, sac.n)
                        VV.wait_ge(sht.h, h16_free[hp])
                        i = VV.tensor_tensor(
                            out=h16[hp][0:M, :],
                            in0=gsil[0:M, kq * 1024 + th * 512: kq * 1024 + th * 512 + 512],
                            in1=up16[sl4][0:M, :], op=ALU.mult)
                        sdv.inc(i)
                        hm_dv[kq * 2 + th] = sdv.n
                        up16_free[sl4] = sdv.n
                        YY.wait_ge(sdv.h, sdv.n)
                        i = YY.dma_start(
                            out=hT_dram[ci * 128: ci * 128 + M, th * 512:(th + 1) * 512],
                            in_=h16[hp][0:M, :])
                        sht.inc(i, 16)
                        h16_free[hp] = sht.n
                ubuf_pe = spe.n
                if jc < 5:
                    VV.wait_ge(spe.h, ubuf_pe)
                    u_rdy[jc + 1] = decode_range(ubuf, IF + (jc + 1) * 1024, OWS[jc + 1])

            YY.wait_ge(swd.h, swd.n)
            hts_free = [spe.n, spe.n]
            wds_free = [spe.n, spe.n]
            ht_all = sht.n
            for o2s in range(4):
                nci = 0
                for wcp in range(6):
                    rows = 64 if wcp == 5 else 128
                    p2 = wcp % 2
                    YY.wait_ge(spe.h, wds_free[p2])
                    srcw = wd_dec[wcp * 512:(wcp + 1) * 512,
                                  o2s * 1024:(o2s + 1) * 1024].rearrange("(c p) o -> p c o", p=128)
                    i = YY.dma_start(out=wds[p2].rearrange("p (c o) -> p c o", c=4), in_=srcw)
                    swr[p2].inc(i, 16)
                    GG.wait_ge(spe.h, hts_free[p2])
                    srch = hT_dram[wcp * 1024:(wcp + 1) * 1024, :].rearrange("(c p) t -> p c t", p=128)
                    i = GG.dma_start(out=hts[p2].rearrange("p (c t) -> p c t", c=8), in_=srch)
                    shr[p2].inc(i, 16)
                    TT.wait_ge(shr[p2].h, shr[p2].n)
                    TT.wait_ge(swr[p2].h, swr[p2].n)
                    if o2s == 0 and wcp == 0:
                        TT.wait_ge(sht.h, ht_all)
                    for kp in range(4):
                        for ln in range(2):
                            kk = kp + 4 * ln
                            for tcc in range(NTC):
                                bk = tcc
                                if nci == 0:
                                    pe_acquire(bk)
                                i = TT.matmul(
                                    pb[bk][:],
                                    hts[p2][0:rows, kk * 1024 + tcc * 128: kk * 1024 + tcc * 128 + 128],
                                    wds[p2][0:rows, kp * 1024 + ln: kp * 1024 + 1024: 2],
                                    start=(nci == 0), stop=(nci == 47),
                                    skip_group_check=True)
                            nci += 1
                    spe.inc(i)
                    hts_free[p2] = spe.n
                    wds_free[p2] = spe.n
                for tcc in range(NTC):
                    bank_done(tcc)
                AA.wait_ge(spe.h, spe.n)
                VV.wait_ge(spe.h, spe.n)
                for tcc in range(NTC):
                    if tcc % 2 == 0:
                        i = AA.copy(out=y2[:, tcc * D + o2s * 512: tcc * D + o2s * 512 + 512],
                                    in_=pb[tcc][:])
                        sac.inc(i)
                        bankR[tcc] = (sac, sac.n)
                    else:
                        i = VV.tensor_copy(out=y2[:, tcc * D + o2s * 512: tcc * D + o2s * 512 + 512],
                                           in_=pb[tcc][:])
                        sdv.inc(i)
                        bankR[tcc] = (sdv, sdv.n)
            y2_dv, y2_ac = sdv.n, sac.n
            if DEBUG:
                YY.wait_ge(sdv.h, y2_dv)
                YY.wait_ge(sac.h, y2_ac)
                i = YY.dma_start(out=dbg_y2, in_=y2); sout.inc(i, 16)

            _scope("P10_ln2")
            def rf(t):
                return r2full[:, t * D:(t + 1) * D]

            YY.wait_ge(sx1.h, sx1.n)
            obuf_free = [0, 0]
            x1r_free = [0, 0]
            AA.wait_ge(sdv.h, y2_dv)
            for tcc in range(NTC):
                AA.activation(out=rf(tcc), in_=y2[:, tcc * D:(tcc + 1) * D], func=AF.Square,
                              accum_out=stats[:, tcc:tcc + 1])
            AA.activation(out=flushd[:], in_=stats[:, 7:8], func=AF.Copy)
            AA.activation(out=flushd[:], in_=stats[:, 7:8], func=AF.Copy)
            AA.activation(out=pst[:, 0:8], in_=stats[:, 0:8], func=AF.Identity, scale=1.0 / 2048)
            AA.activation(out=pst[:, 0:8], in_=pst[:, 0:8], func=AF.Ln, bias=epsrms[:, 0:1])
            i = AA.activation(out=pst[:, 0:8], in_=pst[:, 0:8], func=AF.Exp, scale=-0.5)
            sac.inc(i)
            VV.wait_ge(sac.h, sac.n)
            for tcc in range(NTC):
                par = tcc % 2
                YY.wait_ge(sdv.h, x1r_free[par])
                i = YY.dma_start(out=x1r[par], in_=x1_dram[tcc * 128:(tcc + 1) * 128, :])
                sxr[par].inc(i, 16)
                VV.wait_ge(sxr[par].h, sxr[par].n)
                i = VV.scalar_tensor_tensor(out=rf(tcc), in0=y2[:, tcc * D:(tcc + 1) * D],
                                            scalar=pst[:, tcc:tcc + 1], in1=x1r[par],
                                            op0=ALU.mult, op1=ALU.add)
                sdv.inc(i)
                x1r_free[par] = sdv.n
            r2_dv = sdv.n
            AA.wait_ge(sdv.h, r2_dv)
            for tcc in range(NTC):
                AA.activation(out=obuf[0], in_=rf(tcc), func=AF.Square,
                              accum_out=stats[:, 8 + tcc:9 + tcc])
                AA.activation(out=obuf[0], in_=rf(tcc), func=AF.Identity,
                              accum_out=stats[:, 16 + tcc:17 + tcc])
            AA.activation(out=flushd[:], in_=stats[:, 15:16], func=AF.Copy)
            AA.activation(out=flushd[:], in_=stats[:, 23:24], func=AF.Copy)
            AA.activation(out=pst[:, 8:16], in_=stats[:, 16:24], func=AF.Identity, scale=1.0 / 2048)
            AA.activation(out=pst[:, 16:24], in_=stats[:, 8:16], func=AF.Identity, scale=1.0 / 2048)
            i = AA.activation(out=pst[:, 24:32], in_=pst[:, 8:16], func=AF.Square)
            sac.inc(i)
            VV.wait_ge(sac.h, sac.n)
            i = VV.scalar_tensor_tensor(out=pst[:, 32:40], in0=pst[:, 24:32], scalar=-1.0,
                                        in1=pst[:, 16:24], op0=ALU.mult, op1=ALU.add)
            sdv.inc(i)
            AA.wait_ge(sdv.h, sdv.n)
            AA.activation(out=pst[:, 40:48], in_=pst[:, 32:40], func=AF.Ln, bias=epsln[:, 0:1])
            i = AA.activation(out=pst[:, 48:56], in_=pst[:, 40:48], func=AF.Exp, scale=-0.5)
            sac.inc(i)
            VV.wait_ge(sac.h, sac.n)
            for tcc in range(NTC):
                par = tcc % 2
                VV.wait_ge(sout.h, obuf_free[par])
                i = VV.tensor_scalar(out=obuf[par], in0=rf(tcc), scalar1=pst[:, 8 + tcc:9 + tcc],
                                     scalar2=pst[:, 48 + tcc:49 + tcc],
                                     op0=ALU.subtract, op1=ALU.mult)
                sdv.inc(i)
                YY.wait_ge(sdv.h, sdv.n)
                i = YY.dma_start(out=y_out[tcc * 128:(tcc + 1) * 128, :], in_=obuf[par])
                sout.inc(i, 16)
                obuf_free[par] = sout.n
            YY.wait_ge(sout.h, sout.n)
            GG.wait_ge(sout.h, sout.n)
            _scope(None)

        block.sync(body)

    return nc


def kernel(L0_attn_hidden_states, L0_attn_wq, L0_attn_bq, L0_attn_wk, L0_attn_bk,
           L0_attn_wv, L0_attn_bv, L0_attn_wo, L0_attn_bo, L0_attn_wmap_q,
           L0_attn_wmap_k, L0_ffn_gate_up_packed, L0_ffn_gate_up_scales,
           L0_ffn_down_packed, L0_ffn_down_scales):
    x = np.asarray(L0_attn_hidden_states, dtype=np.float32)
    wq = np.asarray(L0_attn_wq, dtype=np.float32)
    wk = np.asarray(L0_attn_wk, dtype=np.float32)
    wv = np.asarray(L0_attn_wv, dtype=np.float32)
    wo = np.asarray(L0_attn_wo, dtype=np.float32)
    bq = np.asarray(L0_attn_bq, dtype=np.float32)
    bk = np.asarray(L0_attn_bk, dtype=np.float32)
    bv = np.asarray(L0_attn_bv, dtype=np.float32)
    bo = np.asarray(L0_attn_bo, dtype=np.float32)
    wmq = np.asarray(L0_attn_wmap_q, dtype=np.float32)
    wmk = np.asarray(L0_attn_wmap_k, dtype=np.float32)
    gup = np.asarray(L0_ffn_gate_up_packed, dtype=np.int32)
    gus = np.asarray(L0_ffn_gate_up_scales, dtype=np.float16)
    dnp = np.asarray(L0_ffn_down_packed, dtype=np.int32)
    dns = np.asarray(L0_ffn_down_scales, dtype=np.float16)

    wqT = np.ascontiguousarray(wq.T)
    wkT = np.ascontiguousarray(wk.T)
    wvT = np.ascontiguousarray(wv.T)
    woT = np.ascontiguousarray(wo.T)
    bqT = np.ascontiguousarray(bq.reshape(NH, 128).T)
    bkT = np.ascontiguousarray(bk.reshape(NH, 128).T)
    wmqT = np.ascontiguousarray(wmq.T)
    wmkT = np.ascontiguousarray(wmk.T)
    gupT = np.ascontiguousarray(gup.T)
    dnT = np.ascontiguousarray(dnp.T)

    sgu = []
    for wc in range(2):
        t = np.repeat(gus[8 * wc:8 * wc + 8], 16, axis=0)
        sgu.append(np.ascontiguousarray(np.repeat(t, 2, axis=1)))
    sdn = np.zeros((6, 128, 2 * D), dtype=np.float16)
    for wcp in range(6):
        r0, r1 = 8 * wcp, min(8 * wcp + 8, 44)
        t = np.repeat(dns[r0:r1], 16, axis=0)
        sdn[wcp, :t.shape[0], :] = np.repeat(t, 2, axis=1)

    ident = np.eye(128, dtype=np.float16)
    onesr = np.ones((1, 128), dtype=np.float16)

    in_maps = []
    for c in range(8):
        b, hf = c // 2, c % 2
        xs = np.ascontiguousarray(x[b, hf * TOK:(hf + 1) * TOK, :])
        m4 = np.zeros((64, 4), dtype=np.float32)
        m4[:, b] = 1.0
        in_maps.append(dict(
            x_in=xs, wqT=wqT, wkT=wkT, wvT=wvT, woT=woT,
            bqT=bqT, bkT=bkT, bv=bv.reshape(1, D), bo=bo.reshape(1, D),
            wmqT=wmqT, wmkT=wmkT, gupT=gupT, dnT=dnT,
            sgu0=sgu[0], sgu1=sgu[1], sdn=sdn,
            ident=ident, onesr=onesr, mask4p=m4,
        ))
    if _RET_MAPS[0]:
        return in_maps
    nc = build()
    res = run_bass_kernel_spmd(nc, in_maps, core_ids=list(range(8)))
    out = np.empty((B, S, D), dtype=np.float32)
    for c in range(8):
        b, hf = c // 2, c % 2
        out[b, hf * TOK:(hf + 1) * TOK, :] = res.results[c]["y"]
    return out


# version 21 (final)
# speedup vs baseline: 1.3221x; 1.0122x over previous
import numpy as np
from contextlib import ExitStack

import concourse.bass as bass
import concourse.mybir as mybir
from concourse.bass_utils import run_bass_kernel_spmd

F32 = mybir.dt.float32
F16 = mybir.dt.float16
I32 = mybir.dt.int32
I16 = mybir.dt.int16
I8 = mybir.dt.int8
AF = mybir.ActivationFunctionType
ALU = mybir.AluOpType

B, S, D = 4, 2048, 2048
TOK = 1024
NH, DH, DM = 16, 128, 64
IF = 5632
GUO = 2 * IF
WGU = 256
WDN = 704
NCH = 16
NTC = 8
KB = 1024

DEBUG = False
_RET_MAPS = [False]


class Sem:
    __slots__ = ("h", "n")

    def __init__(self, stk, nc, name):
        self.h = stk.enter_context(nc.semaphore(name))
        self.n = 0

    def inc(self, instr, k=1):
        instr.then_inc(self.h, k)
        self.n += k
        return self.n


def build():
    nc = bass.Bass("TRN2", target_bir_lowering=False)

    x_in = nc.dram_tensor("x_in", [TOK, D], F32, kind="ExternalInput").ap()
    wqT_d = nc.dram_tensor("wqT", [D, D], F32, kind="ExternalInput").ap()
    wkT_d = nc.dram_tensor("wkT", [D, D], F32, kind="ExternalInput").ap()
    wvT_d = nc.dram_tensor("wvT", [D, D], F32, kind="ExternalInput").ap()
    woT_d = nc.dram_tensor("woT", [D, D], F32, kind="ExternalInput").ap()
    bqT_d = nc.dram_tensor("bqT", [128, NH], F32, kind="ExternalInput").ap()
    bkT_d = nc.dram_tensor("bkT", [128, NH], F32, kind="ExternalInput").ap()
    bv_d = nc.dram_tensor("bv", [1, D], F32, kind="ExternalInput").ap()
    bo_d = nc.dram_tensor("bo", [1, D], F32, kind="ExternalInput").ap()
    wmqT_d = nc.dram_tensor("wmqT", [128, DM], F32, kind="ExternalInput").ap()
    wmkT_d = nc.dram_tensor("wmkT", [128, DM], F32, kind="ExternalInput").ap()
    gupT_d = nc.dram_tensor("gupT", [WGU, GUO], I32, kind="ExternalInput").ap()
    dnT_d = nc.dram_tensor("dnT", [WDN, D], I32, kind="ExternalInput").ap()
    sgu_d = [nc.dram_tensor("sgu%d" % w, [128, 2 * GUO], F16, kind="ExternalInput").ap()
             for w in range(2)]
    sdn_d = nc.dram_tensor("sdn", [6, 128, 2 * D], F16, kind="ExternalInput").ap()
    ident_d = nc.dram_tensor("ident", [128, 128], F16, kind="ExternalInput").ap()
    ones_d = nc.dram_tensor("onesr", [1, 128], F16, kind="ExternalInput").ap()
    mask_d = nc.dram_tensor("mask4p", [64, 4], F32, kind="ExternalInput").ap()

    y_out = nc.dram_tensor("y", [TOK, D], F32, kind="ExternalOutput").ap()

    kscr = "ExternalOutput" if DEBUG else "Internal"
    x1_dram = nc.dram_tensor("x1_dram", [TOK, D], F32, kind=kscr).ap()
    wd_dec = nc.dram_tensor("wd_dec", [24 * 128, 2 * D], F16, kind=kscr).ap()
    hT_dram = nc.dram_tensor("hT_dram", [48 * 128, TOK], F16, kind=kscr).ap()
    cc_in = nc.dram_tensor("cc_in", [64, 4 * NH * 129], F32, kind="Internal").ap()
    cc_out = nc.dram_tensor("cc_out", [64, 4 * NH * 129], F32, kind="Internal",
                            addr_space="Shared").ap()
    if DEBUG:
        dbg_kvfull = nc.dram_tensor("dbg_kvfull", [64, NH * 129], F32, kind="ExternalOutput").ap()
        dbg_ln = nc.dram_tensor("dbg_ln", [128, 32], F32, kind="ExternalOutput").ap()
        dbg_rbuf = nc.dram_tensor("dbg_rbuf", [128, D], F32, kind="ExternalOutput").ap()
        dbg_y2 = nc.dram_tensor("dbg_y2", [128, NTC * D], F32, kind="ExternalOutput").ap()
        dbg_kv = nc.dram_tensor("dbg_kv", [64, NH * 129], F32, kind="ExternalOutput").ap()
        dbg_att = nc.dram_tensor("dbg_att", [128, NH * TOK], F16, kind="ExternalOutput").ap()
        dbg_km = nc.dram_tensor("dbg_km", [128, NH * 512], F16, kind="ExternalOutput").ap()
        dbg_qm = nc.dram_tensor("dbg_qm", [64, NH * TOK], F16, kind="ExternalOutput").ap()

    TT, VV, AA, GG, YY = nc.tensor, nc.vector, nc.scalar, nc.gpsimd, nc.sync

    with ExitStack() as top:
        E = top.enter_context
        ident = E(nc.sbuf_tensor("identt", [128, 128], F16))
        onesr = E(nc.sbuf_tensor("onest", [1, 128], F16))
        onescol = E(nc.sbuf_tensor("onescol", [128, 1], F16))
        bqT = E(nc.sbuf_tensor("bqTt", [128, NH], F32))
        bkT = E(nc.sbuf_tensor("bkTt", [128, NH], F32))
        bv16 = E(nc.sbuf_tensor("bv16", [1, D], F16))
        bo16 = E(nc.sbuf_tensor("bo16", [1, D], F16))
        wmq16 = E(nc.sbuf_tensor("wmq16", [128, DM], F16))
        wmk16 = E(nc.sbuf_tensor("wmk16", [128, DM], F16))
        mask4p = E(nc.sbuf_tensor("mask4pt", [64, 4], F32))
        epsatt = E(nc.sbuf_tensor("epsatt", [1, 1], F32))
        EPSATT = 1e-6 * (2.0 ** -12)
        nc.const_aps.register(F32, EPSATT, epsatt[0:1, 0:1])
        epsln = E(nc.sbuf_tensor("epsln", [128, 1], F32))
        epsrms = E(nc.sbuf_tensor("epsrms", [128, 1], F32))
        stats = E(nc.sbuf_tensor("stats", [128, 24], F32))
        mv = E(nc.sbuf_tensor("mv", [128, 2], F32))
        mtmp = E(nc.sbuf_tensor("mtmp", [128, 4], F32))
        flushd = E(nc.sbuf_tensor("flushd", [128, 1], F32))
        pst = E(nc.sbuf_tensor("pst", [128, 64], F32))
        cln16 = E(nc.sbuf_tensor("cln16", [128, 1], F32))
        LN16 = float(-4.0 * np.log(2.0))
        nc.const_aps.register(F32, LN16, cln16[:, 0:1])
        arena = E(nc.sbuf_tensor("arena", [128, 192 * KB], I8))

        def av(off, dt, cols, parts=128):
            b = mybir.dt.size(dt) * cols
            return arena[0:parts, off:off + b].bitcast(dt)

        xT = av(0, F16, NCH * TOK)
        dnpk = av(32 * KB, I32, D)
        dnsc = av(40 * KB, F16, 2 * D)
        dnm = av(48 * KB, I16, 2 * D)
        dnsg = av(56 * KB, I16, 2 * D)
        dno = av(64 * KB, F16, 2 * D)
        elutmp = av(72 * KB, F32, KB)
        xh = [av(76 * KB, F16, D), av(80 * KB, F16, D)]
        kqT = [av(84 * KB, F16, 4 * TOK), av(92 * KB, F16, 4 * TOK)]
        km16 = av(100 * KB, F16, NH * 512)
        vgrp = av(116 * KB, F16, NTC * 512)
        ccpart = av(124 * KB, F32, NH * 129, 64)
        qm16 = av(100 * KB, F16, NH * TOK, 64)
        wsl = [av(133 * KB, F16, NCH * 512), av(149 * KB, F16, NCH * 512)]
        cctmp = av(164 * KB + 512, F32, NH * 129, 64)
        kvfull = av(173 * KB, F32, NH * 129, 64)
        kv16 = av(181 * KB + 512, F16, NH * 128, 64)
        ksum16 = av(185 * KB + 512, F16, NH, 64)
        rn = av(164 * KB + 512, F32, 2 * TOK, 1)
        rn16 = [av(186 * KB, F16, 2 * TOK, 1),
                av(96 * KB, F16, 2 * TOK, 1)]
        wofull = av(32 * KB, F16, NCH * D)
        attT = av(133 * KB, F16, NH * TOK)
        xrel = [av(100 * KB, F32, D), av(108 * KB, F32, D)]
        rbuf = av(116 * KB, F32, D)
        x1f = av(124 * KB, F32, D)
        x1h = av(96 * KB, F16, D)
        rstd4 = mtmp[:, 3:4]
        gpk = [av(165 * KB, I32, KB), av(169 * KB, I32, KB)]
        gsc = [av(173 * KB, F16, 2 * KB), av(177 * KB, F16, 2 * KB)]
        gbuf = av(32 * KB, F16, 8 * 2048)
        ubuf = av(64 * KB, F16, 8 * 2048)
        gsil = av(96 * KB, F16, 8 * KB)
        gtm = av(112 * KB, I16, 2 * KB)
        gts = av(116 * KB, I16, 2 * KB)
        up16 = [av((120 + j) * KB, F16, 512) for j in range(4)]
        h16 = [av(124 * KB, F16, 512), av(125 * KB, F16, 512)]
        y2 = av(32 * KB, F32, NTC * D)
        hts = [av(96 * KB, F16, 8 * KB), av(112 * KB, F16, 8 * KB)]
        wds = [av(128 * KB, F16, 4 * KB), av(136 * KB, F16, 4 * KB)]
        r2full = av(96 * KB, F32, NTC * D)
        x1r = [av(160 * KB, F32, D), av(168 * KB, F32, D)]
        obuf = [av(176 * KB, F32, D), av(184 * KB, F32, D)]

        pb = [E(nc.psum_tensor("pb%d" % i, [128, 512], F32)) for i in range(8)]

        sx = Sem(top, nc, "sx")
        sg = Sem(top, nc, "sg")
        sw = [Sem(top, nc, "sw%d" % i) for i in range(2)]
        spk = [Sem(top, nc, "spk%d" % i) for i in range(2)]
        sdnl = Sem(top, nc, "sdnl")
        spe = Sem(top, nc, "spe")
        sdv = Sem(top, nc, "sdv")
        sac = Sem(top, nc, "sac")
        scc = Sem(top, nc, "scc")
        scin = Sem(top, nc, "scin")
        swd = Sem(top, nc, "swd")
        sht = Sem(top, nc, "sht")
        sx1 = Sem(top, nc, "sx1")
        sout = Sem(top, nc, "sout")
        sxr = [Sem(top, nc, "sxr%d" % i) for i in range(2)]
        shr = [Sem(top, nc, "shr%d" % i) for i in range(2)]
        swr = [Sem(top, nc, "swr%d" % i) for i in range(2)]

        bankW = [0] * 8
        bankR = [None] * 8

        def pe_acquire(i):
            if bankR[i] is not None:
                s, v = bankR[i]
                TT.wait_ge(s.h, v)
                bankR[i] = None

        def bank_done(i):
            bankW[i] = spe.n

        def rd_wait(eng, i):
            eng.wait_ge(spe.h, bankW[i])

        block = E(nc.Block())
        _scope_state = [None]

        def _scope(name):
            if _scope_state[0] is not None:
                n, sid = _scope_state[0]
                nc.leave_named_scope(n, sid, False)
                _scope_state[0] = None
            if name is not None:
                sid = nc.enter_named_scope(name, False)[0]
                _scope_state[0] = (name, sid)

        def body(_):
            YY.dma_start(out=ident[:], in_=ident_d).then_inc(sx.h, 16); sx.n += 16
            YY.dma_start(out=onesr[:], in_=ones_d).then_inc(sx.h, 16); sx.n += 16
            YY.dma_start(out=bqT[:], in_=bqT_d).then_inc(sx.h, 16); sx.n += 16
            YY.dma_start(out=bkT[:], in_=bkT_d).then_inc(sx.h, 16); sx.n += 16
            YY.dma_start(out=mask4p[:], in_=mask_d).then_inc(sx.h, 16); sx.n += 16
            consts_x = sx.n
            GG.dma_start(out=bv16[:], in_=bv_d).then_inc(sg.h, 16); sg.n += 16
            GG.dma_start(out=bo16[:], in_=bo_d).then_inc(sg.h, 16); sg.n += 16
            GG.dma_start(out=wmq16[:], in_=wmqT_d).then_inc(sg.h, 16); sg.n += 16
            GG.dma_start(out=wmk16[:], in_=wmkT_d).then_inc(sg.h, 16); sg.n += 16
            consts_g = sg.n
            VV.memset(onescol[:], 1.0)
            VV.memset(epsatt[:], 1e-6 * (2.0 ** -12))
            VV.memset(epsln[:], 1e-5)
            VV.memset(epsrms[:], 1e-6 * (2.0 ** -8))
            i = VV.memset(cln16[:], LN16)
            sdv.inc(i)

            dnstate = {"wcp": 0, "kp": 0, "last_dv": 0, "pk_rdy": 0}

            def dn_unit():
                wcp, kp = dnstate["wcp"], dnstate["kp"]
                if wcp >= 6:
                    return
                rows = 64 if wcp == 5 else 128
                if kp == 0:
                    YY.wait_ge(sdv.h, dnstate["last_dv"])
                    i = YY.dma_start(out=av(32 * KB, I32, D, rows),
                                     in_=dnT_d[wcp * 128:wcp * 128 + rows, :])
                    sdnl.inc(i, 16)
                    i = YY.dma_start(out=av(40 * KB, F16, 2 * D, rows), in_=sdn_d[wcp, 0:rows, :])
                    sdnl.inc(i, 16)
                    VV.wait_ge(sdnl.h, sdnl.n)
                    VV.tensor_scalar(out=av(40 * KB, F16, 2 * D, rows),
                                     in0=av(40 * KB, F16, 2 * D, rows),
                                     scalar1=float(2.0 ** 14), scalar2=None, op0=ALU.mult)
                    dnstate["pk_rdy"] = sdnl.n
                rows_ = rows
                pk16 = av(32 * KB, I32, D, rows_).bitcast(I16)
                tm = av(48 * KB, I16, 2 * D, rows_)
                tsg = av(56 * KB, I16, 2 * D, rows_)
                to = av(64 * KB, F16, 2 * D, rows_)
                sc = av(40 * KB, F16, 2 * D, rows_)
                sm = 9 - 4 * kp
                if sm >= 0:
                    VV.tensor_scalar(out=tm, in0=pk16, scalar1=sm, scalar2=0x0E00,
                                     op0=ALU.logical_shift_left, op1=ALU.bitwise_and)
                else:
                    VV.tensor_scalar(out=tm, in0=pk16, scalar1=-sm, scalar2=0x0E00,
                                     op0=ALU.logical_shift_right, op1=ALU.bitwise_and)
                ss = 12 - 4 * kp
                if ss > 0:
                    VV.tensor_scalar(out=tsg, in0=pk16, scalar1=ss, scalar2=-32768,
                                     op0=ALU.logical_shift_left, op1=ALU.bitwise_and)
                else:
                    VV.tensor_scalar(out=tsg, in0=pk16, scalar1=-32768, scalar2=None,
                                     op0=ALU.bitwise_and)
                VV.tensor_tensor(out=tm, in0=tm, in1=tsg, op=ALU.bitwise_or)
                VV.wait_ge(swd.h, swd.n)
                i = VV.tensor_tensor(out=to, in0=tm.bitcast(F16), in1=sc, op=ALU.mult)
                sdv.inc(i)
                dnstate["last_dv"] = sdv.n
                YY.wait_ge(sdv.h, sdv.n)
                r0 = (wcp * 4 + kp) * 128
                i = YY.dma_start(out=wd_dec[r0:r0 + rows_, :], in_=to)
                swd.inc(i, 16)
                if kp == 3:
                    dnstate["wcp"] += 1
                    dnstate["kp"] = 0
                else:
                    dnstate["kp"] = kp + 1

            _scope("P1_xT")
            xT3 = xT.rearrange("p (c t) -> p c t", c=NCH)
            xh_free = [0, 0]
            xh_rdy = [0, 0]
            for tc in range(NTC):
                par = tc % 2
                GG.wait_ge(spe.h, xh_free[par])
                i = GG.dma_start(out=xh[par], in_=x_in[tc * 128:(tc + 1) * 128, :])
                sg.inc(i, 16)
                xh_rdy[par] = sg.n
                TT.wait_ge(sg.h, xh_rdy[par])
                if tc == 0:
                    TT.wait_ge(sx.h, consts_x)
                for half in range(2):
                    bk = (2 * tc) % 4 + half
                    pe_acquire(bk)
                    pv = pb[bk][:].bitcast(F16)
                    for j in range(8):
                        ic = half * 8 + j
                        i = TT.transpose(pv[:, j * 128:(j + 1) * 128],
                                         xh[par][:, ic * 128:(ic + 1) * 128], ident[:])
                    spe.inc(i)
                    bank_done(bk)
                    eng, seng = (VV, sdv) if half == 0 else (AA, sac)
                    rd_wait(eng, bk)
                    dst = xT3[:, half * 8:(half + 1) * 8, tc * 128:(tc + 1) * 128]
                    src = pv[:, 0:1024].rearrange("p (c t) -> p c t", c=8)
                    if half == 0:
                        i = VV.tensor_copy(out=dst, in_=src)
                    else:
                        i = AA.copy(out=dst, in_=src)
                    seng.inc(i)
                    bankR[bk] = (seng, seng.n)
                xh_free[par] = spe.n
                dn_unit()
            xT_dv, xT_ac = sdv.n, sac.n

            slab_free = [0, 0]
            slab_rdy = [0, 0]
            kq_free = [0, 0]

            def emit_proj(wT_d, kind, biasT=None):
                first_call_wait = (kind == "k")
                for sb in range(4):
                    par = sb % 2
                    GG.wait_ge(spe.h, slab_free[par])
                    src = wT_d.rearrange("(c p) o -> p c o", p=128)[:, :, sb * 512:(sb + 1) * 512]
                    dstw = wsl[par].rearrange("p (c o) -> p c o", c=NCH)
                    i = GG.dma_start(out=dstw, in_=src)
                    sw[par].inc(i, 16)
                    slab_rdy[par] = sw[par].n

                    TT.wait_ge(sw[par].h, slab_rdy[par])
                    if sb == 0 and first_call_wait:
                        TT.wait_ge(sdv.h, xT_dv)
                        TT.wait_ge(sac.h, xT_ac)
                    if kind in ("k", "q"):
                        for ic in range(NCH):
                            for h4 in range(4):
                                for th2 in range(2):
                                    bk = h4 * 2 + th2
                                    if ic == 0:
                                        pe_acquire(bk)
                                    i = TT.matmul(
                                        pb[bk][:],
                                        wsl[par][:, ic * 512 + h4 * 128: ic * 512 + h4 * 128 + 128],
                                        xT[:, ic * TOK + th2 * 512: ic * TOK + th2 * 512 + 512],
                                        start=(ic == 0), stop=(ic == NCH - 1))
                        spe.inc(i)
                        for bk in range(8):
                            bank_done(bk)
                        slab_free[par] = spe.n
                        kqb = kqT[par]
                        VV.wait_ge(spe.h, spe.n)
                        VV.wait_ge(spe.h, kq_free[par])
                        if sb == 0:
                            VV.wait_ge(sx.h, consts_x)
                        for h4 in range(4):
                            for th2 in range(2):
                                bk = h4 * 2 + th2
                                i = VV.tensor_scalar(
                                    out=kqb[:, h4 * TOK + th2 * 512: h4 * TOK + th2 * 512 + 512],
                                    in0=pb[bk][:], scalar1=biasT[:, sb * 4 + h4: sb * 4 + h4 + 1],
                                    scalar2=None, op0=ALU.add)
                                sdv.inc(i)
                                bankR[bk] = (sdv, sdv.n)
                        evac_dv = sdv.n
                        wmap = wmk16 if kind == "k" else wmq16
                        TT.wait_ge(sdv.h, evac_dv)
                        if sb == 0 and kind == "k":
                            TT.wait_ge(sg.h, consts_g)
                        for h4 in range(4):
                            h = sb * 4 + h4
                            if kind == "k":
                                bk = h4 * 2
                                pe_acquire(bk)
                                for tcc in range(NTC):
                                    i = TT.matmul(
                                        pb[bk][:, tcc * 64:(tcc + 1) * 64],
                                        kqb[:, h4 * TOK + tcc * 128: h4 * TOK + tcc * 128 + 128],
                                        wmap[:], start=(tcc == 0), stop=(tcc == NTC - 1),
                                        skip_group_check=True)
                                spe.inc(i)
                                bank_done(bk)
                                rd_wait(VV, bk)
                                i = VV.tensor_scalar(out=elutmp[:, 0:512], in0=pb[bk][:],
                                                     scalar1=0.0, scalar2=None, op0=ALU.min)
                                sdv.inc(i)
                                rd_wait(AA, bk)
                                AA.wait_ge(sdv.h, sdv.n)
                                i = AA.activation(out=elutmp[:, 512:1024], in_=pb[bk][:], func=AF.Relu)
                                sac.inc(i)
                                bankR[bk] = (sac, sac.n)
                                i = AA.activation(out=elutmp[:, 0:512], in_=elutmp[:, 0:512], func=AF.Exp)
                                sac.inc(i)
                                VV.wait_ge(sac.h, sac.n)
                                i = VV.tensor_tensor(out=km16[:, h * 512:(h + 1) * 512],
                                                     in0=elutmp[:, 0:512], in1=elutmp[:, 512:1024],
                                                     op=ALU.add)
                                sdv.inc(i)
                            else:
                                for th2 in range(2):
                                    bk = h4 * 2 + th2
                                    pe_acquire(bk)
                                    i = TT.matmul(
                                        pb[bk][0:64, :], wmap[:],
                                        kqb[:, h4 * TOK + th2 * 512: h4 * TOK + th2 * 512 + 512],
                                        start=True, stop=True)
                                    spe.inc(i)
                                    bank_done(bk)
                                    rd_wait(VV, bk)
                                    i = VV.tensor_scalar(out=elutmp[0:64, 0:512], in0=pb[bk][0:64, :],
                                                         scalar1=0.0, scalar2=None, op0=ALU.min)
                                    sdv.inc(i)
                                    rd_wait(AA, bk)
                                    AA.wait_ge(sdv.h, sdv.n)
                                    i = AA.activation(out=elutmp[0:64, 512:1024], in_=pb[bk][0:64, :],
                                                      func=AF.Relu)
                                    sac.inc(i)
                                    bankR[bk] = (sac, sac.n)
                                    i = AA.activation(out=elutmp[0:64, 0:512], in_=elutmp[0:64, 0:512],
                                                      func=AF.Exp)
                                    sac.inc(i)
                                    VV.wait_ge(sac.h, sac.n)
                                    i = VV.tensor_tensor(
                                        out=qm16[:, h * TOK + th2 * 512: h * TOK + th2 * 512 + 512],
                                        in0=elutmp[0:64, 0:512], in1=elutmp[0:64, 512:1024],
                                        op=ALU.add)
                                    sdv.inc(i)
                        kq_free[par] = spe.n
                        dn_unit()
                    else:
                        if sb == 0:
                            TT.wait_ge(sx.h, consts_x)
                            TT.wait_ge(sg.h, consts_g)
                            TT.wait_ge(sdv.h, sdv.n)
                        pe_acquire(4)
                        pe_acquire(5)
                        for tcg in range(2):
                            for ic in range(NCH):
                                for t4 in range(4):
                                    tcc = tcg * 4 + t4
                                    bk = t4
                                    if ic == 0:
                                        pe_acquire(bk)
                                    i = TT.matmul(
                                        pb[bk][:],
                                        xT[:, ic * TOK + tcc * 128: ic * TOK + tcc * 128 + 128],
                                        wsl[par][:, ic * 512: ic * 512 + 512],
                                        start=(ic == 0), stop=False)
                            for t4 in range(4):
                                i = TT.matmul(pb[t4][:], onesr[0:1, 0:128],
                                              bv16[0:1, sb * 512:(sb + 1) * 512],
                                              start=False, stop=True)
                            spe.inc(i)
                            for t4 in range(4):
                                bank_done(t4)
                            AA.wait_ge(spe.h, spe.n)
                            for t4 in range(4):
                                tcc = tcg * 4 + t4
                                i = AA.copy(out=vgrp[:, tcc * 512:(tcc + 1) * 512], in_=pb[t4][:])
                                sac.inc(i)
                                bankR[t4] = (sac, sac.n)
                            TT.wait_ge(sac.h, sac.n)
                            for h4 in range(4):
                                h = sb * 4 + h4
                                for t4 in range(4):
                                    tcc = tcg * 4 + t4
                                    st = (tcg == 0 and h4 == 0 and t4 == 0)
                                    sp = (tcg == 1 and h4 == 3 and t4 == 3)
                                    i = TT.matmul(
                                        pb[4][0:64, h4 * 128:(h4 + 1) * 128],
                                        km16[:, h * 512 + tcc * 64: h * 512 + tcc * 64 + 64],
                                        vgrp[:, tcc * 512 + h4 * 128: tcc * 512 + h4 * 128 + 128],
                                        start=st, stop=sp, skip_group_check=True)
                                    i = TT.matmul(
                                        pb[5][0:64, h:h + 1],
                                        km16[:, h * 512 + tcc * 64: h * 512 + tcc * 64 + 64],
                                        onescol[:], start=st, stop=sp, skip_group_check=True)
                        spe.inc(i)
                        bank_done(4)
                        bank_done(5)
                        slab_free[par] = spe.n
                        VV.wait_ge(spe.h, spe.n)
                        for h4 in range(4):
                            h = sb * 4 + h4
                            i = VV.tensor_copy(out=ccpart[:, h * 129: h * 129 + 128],
                                               in_=pb[4][0:64, h4 * 128:(h4 + 1) * 128])
                            sdv.inc(i)
                            i = VV.tensor_copy(out=ccpart[:, h * 129 + 128: h * 129 + 129],
                                               in_=pb[5][0:64, h:h + 1])
                            sdv.inc(i)
                        bankR[4] = (sdv, sdv.n)
                        bankR[5] = (sdv, sdv.n)
                        dn_unit()
                        dn_unit()

            _scope("P2_K")
            emit_proj(wkT_d, "k", biasT=bkT)
            if DEBUG:
                YY.wait_ge(sdv.h, sdv.n)
                i = YY.dma_start(out=dbg_km, in_=km16); sout.inc(i, 16)
            _scope("P3_V")
            emit_proj(wvT_d, "v")
            while dnstate["wcp"] < 6:
                dn_unit()
            _scope("P4_cc")
            for b in range(4):
                i = VV.tensor_scalar(out=cctmp, in0=ccpart, scalar1=mask4p[0:64, b:b + 1],
                                     scalar2=None, op0=ALU.mult)
                sdv.inc(i)
                YY.wait_ge(sdv.h, sdv.n)
                i = YY.dma_start(out=cc_in[:, b * NH * 129:(b + 1) * NH * 129], in_=cctmp)
                scin.inc(i, 16)
                if b < 3:
                    VV.wait_ge(scin.h, scin.n)
            GG.wait_ge(scin.h, scin.n)
            i = GG.collective_compute("AllReduce", ALU.add,
                                      replica_groups=[[0, 1, 2, 3, 4, 5, 6, 7]],
                                      ins=[cc_in.opt()], outs=[cc_out.opt()])
            i.then_inc(scc.h)
            scc.n = 1
            if DEBUG:
                YY.wait_ge(sdv.h, sdv.n)
                i = YY.dma_start(out=dbg_kv, in_=ccpart); sout.inc(i, 16)

            _scope("P5_Q")
            emit_proj(wqT_d, "q", biasT=bqT)
            qm_dv = sdv.n
            if DEBUG:
                YY.wait_ge(sdv.h, qm_dv)
                i = YY.dma_start(out=dbg_qm, in_=qm16); sout.inc(i, 16)
                VV.wait_ge(sout.h, sout.n)

            GG.wait_ge(spe.h, kq_free[0])
            GG.wait_ge(spe.h, kq_free[1])
            GG.wait_ge(sdv.h, qm_dv)
            GG.wait_ge(sac.h, sac.n)
            src = woT_d.rearrange("(c p) o -> p c o", p=128)
            i = GG.dma_start(out=wofull.rearrange("p (c o) -> p c o", c=NCH), in_=src)
            sg.inc(i, 16)
            wo_rdy = sg.n

            _scope("P6_att")
            GG.wait_ge(scc.h, 1)
            for b in range(4):
                GG.wait_ge(sdv.h, sdv.n)
                i = GG.dma_start(out=cctmp, in_=cc_out[:, b * NH * 129:(b + 1) * NH * 129])
                sg.inc(i, 16)
                VV.wait_ge(sg.h, sg.n)
                if b == 0:
                    i = VV.tensor_scalar(out=kvfull, in0=cctmp, scalar1=mask4p[0:64, 0:1],
                                         scalar2=None, op0=ALU.mult)
                else:
                    i = VV.scalar_tensor_tensor(out=kvfull, in0=cctmp,
                                                scalar=mask4p[0:64, b:b + 1], in1=kvfull,
                                                op0=ALU.mult, op1=ALU.add)
                sdv.inc(i)
            for h in range(NH):
                VV.tensor_scalar(out=kv16[:, h * 128:(h + 1) * 128],
                                 in0=kvfull[:, h * 129:h * 129 + 128],
                                 scalar1=float(2.0 ** -12), scalar2=None, op0=ALU.mult)
            i = VV.tensor_copy(out=ksum16, in_=kvfull[:, 128:NH * 129:129])
            sdv.inc(i)
            kv_rdy = sdv.n
            if DEBUG:
                YY.wait_ge(sdv.h, kv_rdy)
                i = YY.dma_start(out=dbg_kvfull, in_=kvfull); sout.inc(i, 16)

            rn_dv = [0] * 8
            qmn_dv = [0] * 8
            TT.wait_ge(sdv.h, kv_rdy)
            NG = 8
            for s in range(NG + 3):
                g = s - 3
                if 0 <= g < NG:
                    h0 = g * 2
                    TT.wait_ge(sdv.h, qmn_dv[g])
                    for h4 in range(2):
                        h = h0 + h4
                        for th2 in range(2):
                            bk = 4 + h4 * 2 + th2
                            pe_acquire(bk)
                            i = TT.matmul(pb[bk][:], kv16[:, h * 128:(h + 1) * 128],
                                          qm16[:, h * TOK + th2 * 512: h * TOK + th2 * 512 + 512],
                                          start=True, stop=True)
                            spe.inc(i)
                            bank_done(bk)
                    AA.wait_ge(spe.h, spe.n)
                    for h4 in range(2):
                        h = h0 + h4
                        for th2 in range(2):
                            bk = 4 + h4 * 2 + th2
                            i = AA.copy(out=attT[:, h * TOK + th2 * 512: h * TOK + th2 * 512 + 512],
                                        in_=pb[bk][:])
                            sac.inc(i)
                            bankR[bk] = (sac, sac.n)
                g = s - 2
                if 0 <= g < NG:
                    p2 = g % 2
                    TT.wait_ge(sdv.h, rn_dv[g])
                    for h4 in range(2):
                        bk = 2 + h4
                        pe_acquire(bk)
                        for th2 in range(2):
                            i = TT.matmul(pb[bk][64 * th2:64 * th2 + 64, :], onesr[0:1, 0:64],
                                          rn16[p2][0:1, (h4 * 2 + th2) * 512:(h4 * 2 + th2) * 512 + 512],
                                          start=True, stop=True, skip_group_check=True)
                        spe.inc(i)
                        bank_done(bk)
                    rd_wait(VV, 2)
                    rd_wait(VV, 3)
                    h0 = g * 2
                    for h4 in range(2):
                        h = h0 + h4
                        for th2 in range(2):
                            i = VV.tensor_tensor(
                                out=qm16[:, h * TOK + th2 * 512: h * TOK + th2 * 512 + 512],
                                in0=qm16[:, h * TOK + th2 * 512: h * TOK + th2 * 512 + 512],
                                in1=pb[2 + h4][64 * th2:64 * th2 + 64, :], op=ALU.mult)
                            sdv.inc(i)
                    bankR[2] = (sdv, sdv.n)
                    bankR[3] = (sdv, sdv.n)
                    qmn_dv[g] = sdv.n
                g = s - 1
                if 0 <= g < NG:
                    p2 = g % 2
                    rd_wait(AA, 0)
                    rd_wait(AA, 1)
                    if g >= 1:
                        AA.wait_ge(sdv.h, rn_dv[g - 1])
                    for h4 in range(2):
                        for th2 in range(2):
                            i = AA.activation(
                                out=rn[0:1, (h4 * 2 + th2) * 512:(h4 * 2 + th2) * 512 + 512],
                                in_=pb[h4][64 * th2:64 * th2 + 1, :],
                                func=AF.Copy, scale=float(2.0 ** -12), bias=EPSATT)
                            sac.inc(i)
                    bankR[0] = (sac, sac.n)
                    bankR[1] = (sac, sac.n)
                    VV.wait_ge(sac.h, sac.n)
                    VV.reciprocal(out=rn[0:1, :], in_=rn[0:1, :])
                    i = VV.tensor_copy(out=rn16[p2][0:1, :], in_=rn[0:1, :])
                    sdv.inc(i)
                    rn_dv[g] = sdv.n
                g = s
                if g < NG:
                    h0 = g * 2
                    for h4 in range(2):
                        h = h0 + h4
                        bk = h4
                        pe_acquire(bk)
                        for th2 in range(2):
                            i = TT.matmul(pb[bk][64 * th2:64 * th2 + 1, :], ksum16[:, h:h + 1],
                                          qm16[:, h * TOK + th2 * 512: h * TOK + th2 * 512 + 512],
                                          start=True, stop=True, skip_group_check=True)
                        spe.inc(i)
                        bank_done(bk)
            att_ac = sac.n
            if DEBUG:
                YY.wait_ge(sac.h, att_ac)
                i = YY.dma_start(out=dbg_att, in_=attT); sout.inc(i, 16)

            _scope("P7_Wo")
            x1T3 = xT.rearrange("p (c t) -> p c t", c=NCH)
            TT.wait_ge(sg.h, wo_rdy)
            TT.wait_ge(sac.h, att_ac)
            YY.wait_ge(spe.h, spe.n)
            xrel_free = [0, 0]
            x1h_pe = 0
            for tc in range(NTC):
                par = tc % 2
                YY.wait_ge(sdv.h, xrel_free[par])
                i = YY.dma_start(out=xrel[par], in_=x_in[tc * 128:(tc + 1) * 128, :])
                sxr[par].inc(i, 16)
                for os in range(4):
                    bk = os
                    pe_acquire(bk)
                    for hh in range(NH):
                        i = TT.matmul(pb[bk][:],
                                      attT[:, hh * TOK + tc * 128: hh * TOK + tc * 128 + 128],
                                      wofull[:, hh * D + os * 512: hh * D + os * 512 + 512],
                                      start=(hh == 0), stop=False)
                    i = TT.matmul(pb[bk][:], onesr[0:1, 0:128],
                                  bo16[0:1, os * 512:(os + 1) * 512], start=False, stop=True)
                    spe.inc(i)
                    bank_done(bk)
                VV.wait_ge(sxr[par].h, sxr[par].n)
                VV.wait_ge(spe.h, spe.n)
                for os in range(4):
                    i = VV.tensor_tensor(out=rbuf[:, os * 512:(os + 1) * 512],
                                         in0=xrel[par][:, os * 512:(os + 1) * 512],
                                         in1=pb[os][:], op=ALU.add)
                    sdv.inc(i)
                    bankR[os] = (sdv, sdv.n)
                xrel_free[par] = sdv.n
                AA.wait_ge(sdv.h, sdv.n)
                AA.wait_ge(sx1.h, sx1.n)
                AA.activation(out=x1f, in_=rbuf, func=AF.Square, accum_out=stats[:, 8:9])
                AA.activation(out=x1f, in_=rbuf, func=AF.Identity, accum_out=stats[:, 9:10])
                AA.activation(out=flushd[:], in_=stats[:, 8:9], func=AF.Copy)
                AA.activation(out=flushd[:], in_=stats[:, 9:10], func=AF.Copy)
                AA.activation(out=mv[:, 0:1], in_=stats[:, 9:10], func=AF.Identity, scale=1.0 / 2048)
                AA.activation(out=mtmp[:, 3:4], in_=stats[:, 8:9], func=AF.Identity, scale=1.0 / 2048)
                i = AA.activation(out=mtmp[:, 2:3], in_=mv[:, 0:1], func=AF.Square)
                sac.inc(i)
                VV.wait_ge(sac.h, sac.n)
                i = VV.scalar_tensor_tensor(out=mv[:, 1:2], in0=mtmp[:, 2:3], scalar=-1.0,
                                            in1=mtmp[:, 3:4], op0=ALU.mult, op1=ALU.add)
                sdv.inc(i)
                AA.wait_ge(sdv.h, sdv.n)
                AA.activation(out=mtmp[:, 0:1], in_=mv[:, 1:2], func=AF.Ln, bias=epsln[:, 0:1])
                AA.activation(out=mtmp[:, 1:2], in_=mtmp[:, 0:1], func=AF.Exp, scale=-0.5)
                i = AA.activation(out=rstd4, in_=mtmp[:, 0:1], func=AF.Exp, scale=-0.5, bias=LN16)
                sac.inc(i)
                VV.wait_ge(sac.h, sac.n)
                VV.wait_ge(sx1.h, sx1.n)
                i = VV.tensor_scalar(out=x1f, in0=rbuf, scalar1=mv[:, 0:1],
                                     scalar2=mtmp[:, 1:2], op0=ALU.subtract, op1=ALU.mult)
                sdv.inc(i)
                YY.wait_ge(sdv.h, sdv.n)
                i = YY.dma_start(out=x1_dram[tc * 128:(tc + 1) * 128, :], in_=x1f)
                sx1.inc(i, 16)
                VV.wait_ge(spe.h, x1h_pe)
                i = VV.tensor_scalar(out=x1h, in0=rbuf, scalar1=mv[:, 0:1],
                                     scalar2=rstd4, op0=ALU.subtract, op1=ALU.mult)
                sdv.inc(i)
                x1h_dv = sdv.n
                TT.wait_ge(sdv.h, x1h_dv)
                for half in range(2):
                    bk = 4 + half
                    pe_acquire(bk)
                    pv = pb[bk][:].bitcast(F16)
                    for j in range(8):
                        kidx = half * 8 + j
                        wc, kk = kidx // 8, kidx % 8
                        col0 = 1024 * wc + kk
                        i = TT.transpose(pv[:, j * 128:(j + 1) * 128],
                                         x1h[:, col0:col0 + 1017:8], ident[:])
                    spe.inc(i)
                    bank_done(bk)
                    eng, seng = (VV, sdv) if half == 0 else (AA, sac)
                    rd_wait(eng, bk)
                    dst = x1T3[:, half * 8:(half + 1) * 8, tc * 128:(tc + 1) * 128]
                    srcp = pv[:, 0:1024].rearrange("p (c t) -> p c t", c=8)
                    if half == 0:
                        i = VV.tensor_copy(out=dst, in_=srcp)
                    else:
                        i = AA.copy(out=dst, in_=srcp)
                    seng.inc(i)
                    bankR[bk] = (seng, seng.n)
                x1h_pe = spe.n
                if DEBUG and tc == 7:
                    YY.wait_ge(sdv.h, sdv.n)
                    i = YY.dma_start(out=dbg_ln[:, 0:24], in_=stats[:]); sout.inc(i, 16)
                    i = YY.dma_start(out=dbg_ln[:, 24:26], in_=mv[:]); sout.inc(i, 16)
                    i = YY.dma_start(out=dbg_ln[:, 26:30], in_=mtmp[:]); sout.inc(i, 16)
                    i = YY.dma_start(out=dbg_rbuf, in_=rbuf); sout.inc(i, 16)
            x1T_dv, x1T_ac = sdv.n, sac.n

            _scope("P8_gu")
            ldp = [0]
            gpk_free = [0, 0]

            def decode_range(dstbuf, obase, owidth):
                for wc in range(2):
                    par = ldp[0] % 2
                    ldp[0] += 1
                    YY.wait_ge(sdv.h, gpk_free[par])
                    i = YY.dma_start(out=gpk[par][:, 0:owidth],
                                     in_=gupT_d[wc * 128:(wc + 1) * 128, obase:obase + owidth])
                    spk[par].inc(i, 16)
                    i = YY.dma_start(out=gsc[par][:, 0:2 * owidth],
                                     in_=sgu_d[wc][:, 2 * obase:2 * obase + 2 * owidth])
                    spk[par].inc(i, 16)
                    VV.wait_ge(spk[par].h, spk[par].n)
                    VV.tensor_scalar(out=gsc[par][:, 0:2 * owidth], in0=gsc[par][:, 0:2 * owidth],
                                     scalar1=float(2.0 ** 14), scalar2=None, op0=ALU.mult)
                    pk16 = gpk[par][:, 0:owidth].bitcast(I16)
                    for kp in range(4):
                        sm = 9 - 4 * kp
                        if sm >= 0:
                            VV.tensor_scalar(out=gtm[:, 0:2 * owidth], in0=pk16, scalar1=sm,
                                             scalar2=0x0E00, op0=ALU.logical_shift_left,
                                             op1=ALU.bitwise_and)
                        else:
                            VV.tensor_scalar(out=gtm[:, 0:2 * owidth], in0=pk16, scalar1=-sm,
                                             scalar2=0x0E00, op0=ALU.logical_shift_right,
                                             op1=ALU.bitwise_and)
                        ss = 12 - 4 * kp
                        if ss > 0:
                            VV.tensor_scalar(out=gts[:, 0:2 * owidth], in0=pk16, scalar1=ss,
                                             scalar2=-32768, op0=ALU.logical_shift_left,
                                             op1=ALU.bitwise_and)
                        else:
                            VV.tensor_scalar(out=gts[:, 0:2 * owidth], in0=pk16, scalar1=-32768,
                                             scalar2=None, op0=ALU.bitwise_and)
                        VV.tensor_tensor(out=gtm[:, 0:2 * owidth], in0=gtm[:, 0:2 * owidth],
                                         in1=gts[:, 0:2 * owidth], op=ALU.bitwise_or)
                        ti = wc * 4 + kp
                        i = VV.tensor_tensor(out=dstbuf[:, ti * 2048: ti * 2048 + 2 * owidth],
                                             in0=gtm[:, 0:2 * owidth].bitcast(F16),
                                             in1=gsc[par][:, 0:2 * owidth], op=ALU.mult)
                        sdv.inc(i)
                    gpk_free[par] = sdv.n
                return sdv.n

            gbuf_pe = 0
            ubuf_pe = 0
            h16_free = [0, 0]
            hm_dv = [0] * 16
            up16_free = [0] * 4
            OWS = [1024, 1024, 1024, 1024, 1024, 512]
            g_rdy = [0] * 6
            u_rdy = [0] * 6
            g_rdy[0] = decode_range(gbuf, 0, OWS[0])
            u_rdy[0] = decode_range(ubuf, IF, OWS[0])
            for jc in range(6):
                ow = OWS[jc]
                M = 64 if jc == 5 else 128
                TT.wait_ge(sdv.h, g_rdy[jc])
                if jc == 0:
                    TT.wait_ge(sdv.h, x1T_dv)
                    TT.wait_ge(sac.h, x1T_ac)
                for kq in range(8):
                    for th in range(2):
                        bk = (kq * 2 + th) % 4
                        pe_acquire(bk)
                        first = True
                        for wc in range(2):
                            for kp in range(4):
                                for ln in range(2):
                                    ti = wc * 4 + kp
                                    cix = wc * 8 + kp + 4 * ln
                                    st0 = ti * 2048 + 2 * kq + ln
                                    i = TT.matmul(
                                        pb[bk][0:M, :],
                                        gbuf[:, st0: st0 + 16 * (M - 1) + 1: 16],
                                        xT[:, cix * TOK + th * 512: cix * TOK + th * 512 + 512],
                                        start=first, stop=(wc == 1 and kp == 3 and ln == 1))
                                    first = False
                        spe.inc(i)
                        bank_done(bk)
                        rd_wait(AA, bk)
                        AA.wait_ge(sdv.h, hm_dv[kq * 2 + th])
                        i = AA.activation(
                            out=gsil[0:M, kq * 1024 + th * 512: kq * 1024 + th * 512 + 512],
                            in_=pb[bk][0:M, :], func=AF.Silu, scale=16.0)
                        sac.inc(i)
                        bankR[bk] = (sac, sac.n)
                gbuf_pe = spe.n
                if jc < 5:
                    VV.wait_ge(spe.h, gbuf_pe)
                    g_rdy[jc + 1] = decode_range(gbuf, (jc + 1) * 1024, OWS[jc + 1])
                TT.wait_ge(sdv.h, u_rdy[jc])
                for kq in range(8):
                    ci = jc * 8 + kq
                    for th in range(2):
                        bk = 4 + (kq * 2 + th) % 4
                        pe_acquire(bk)
                        first = True
                        for wc in range(2):
                            for kp in range(4):
                                for ln in range(2):
                                    ti = wc * 4 + kp
                                    cix = wc * 8 + kp + 4 * ln
                                    st0 = ti * 2048 + 2 * kq + ln
                                    i = TT.matmul(
                                        pb[bk][0:M, :],
                                        ubuf[:, st0: st0 + 16 * (M - 1) + 1: 16],
                                        xT[:, cix * TOK + th * 512: cix * TOK + th * 512 + 512],
                                        start=first, stop=(wc == 1 and kp == 3 and ln == 1))
                                    first = False
                        spe.inc(i)
                        bank_done(bk)
                        rd_wait(AA, bk)
                        sl4 = (kq * 2 + th) % 4
                        AA.wait_ge(sdv.h, up16_free[sl4])
                        i = AA.copy(out=up16[sl4][0:M, :], in_=pb[bk][0:M, :])
                        sac.inc(i)
                        bankR[bk] = (sac, sac.n)
                        hp = th
                        VV.wait_ge(sac.h, sac.n)
                        VV.wait_ge(sht.h, h16_free[hp])
                        i = VV.tensor_tensor(
                            out=h16[hp][0:M, :],
                            in0=gsil[0:M, kq * 1024 + th * 512: kq * 1024 + th * 512 + 512],
                            in1=up16[sl4][0:M, :], op=ALU.mult)
                        sdv.inc(i)
                        hm_dv[kq * 2 + th] = sdv.n
                        up16_free[sl4] = sdv.n
                        YY.wait_ge(sdv.h, sdv.n)
                        i = YY.dma_start(
                            out=hT_dram[ci * 128: ci * 128 + M, th * 512:(th + 1) * 512],
                            in_=h16[hp][0:M, :])
                        sht.inc(i, 16)
                        h16_free[hp] = sht.n
                ubuf_pe = spe.n
                if jc < 5:
                    VV.wait_ge(spe.h, ubuf_pe)
                    u_rdy[jc + 1] = decode_range(ubuf, IF + (jc + 1) * 1024, OWS[jc + 1])

            YY.wait_ge(swd.h, swd.n)
            hts_free = [spe.n, spe.n]
            wds_free = [spe.n, spe.n]
            ht_all = sht.n
            for o2s in range(4):
                nci = 0
                for wcp in range(6):
                    rows = 64 if wcp == 5 else 128
                    p2 = wcp % 2
                    YY.wait_ge(spe.h, wds_free[p2])
                    srcw = wd_dec[wcp * 512:(wcp + 1) * 512,
                                  o2s * 1024:(o2s + 1) * 1024].rearrange("(c p) o -> p c o", p=128)
                    i = YY.dma_start(out=wds[p2].rearrange("p (c o) -> p c o", c=4), in_=srcw)
                    swr[p2].inc(i, 16)
                    GG.wait_ge(spe.h, hts_free[p2])
                    srch = hT_dram[wcp * 1024:(wcp + 1) * 1024, :].rearrange("(c p) t -> p c t", p=128)
                    i = GG.dma_start(out=hts[p2].rearrange("p (c t) -> p c t", c=8), in_=srch)
                    shr[p2].inc(i, 16)
                    TT.wait_ge(shr[p2].h, shr[p2].n)
                    TT.wait_ge(swr[p2].h, swr[p2].n)
                    if o2s == 0 and wcp == 0:
                        TT.wait_ge(sht.h, ht_all)
                    for kp in range(4):
                        for ln in range(2):
                            kk = kp + 4 * ln
                            for tcc in range(NTC):
                                bk = tcc
                                if nci == 0:
                                    pe_acquire(bk)
                                i = TT.matmul(
                                    pb[bk][:],
                                    hts[p2][0:rows, kk * 1024 + tcc * 128: kk * 1024 + tcc * 128 + 128],
                                    wds[p2][0:rows, kp * 1024 + ln: kp * 1024 + 1024: 2],
                                    start=(nci == 0), stop=(nci == 47),
                                    skip_group_check=True)
                            nci += 1
                    spe.inc(i)
                    hts_free[p2] = spe.n
                    wds_free[p2] = spe.n
                for tcc in range(NTC):
                    bank_done(tcc)
                AA.wait_ge(spe.h, spe.n)
                VV.wait_ge(spe.h, spe.n)
                for tcc in range(NTC):
                    if tcc % 2 == 0:
                        i = AA.copy(out=y2[:, tcc * D + o2s * 512: tcc * D + o2s * 512 + 512],
                                    in_=pb[tcc][:])
                        sac.inc(i)
                        bankR[tcc] = (sac, sac.n)
                    else:
                        i = VV.tensor_copy(out=y2[:, tcc * D + o2s * 512: tcc * D + o2s * 512 + 512],
                                           in_=pb[tcc][:])
                        sdv.inc(i)
                        bankR[tcc] = (sdv, sdv.n)
            y2_dv, y2_ac = sdv.n, sac.n
            if DEBUG:
                YY.wait_ge(sdv.h, y2_dv)
                YY.wait_ge(sac.h, y2_ac)
                i = YY.dma_start(out=dbg_y2, in_=y2); sout.inc(i, 16)

            _scope("P10_ln2")
            def rf(t):
                return r2full[:, t * D:(t + 1) * D]

            YY.wait_ge(sx1.h, sx1.n)
            obuf_free = [0, 0]
            x1r_free = [0, 0]
            AA.wait_ge(sdv.h, y2_dv)
            for tcc in range(NTC):
                AA.activation(out=rf(tcc), in_=y2[:, tcc * D:(tcc + 1) * D], func=AF.Square,
                              accum_out=stats[:, tcc:tcc + 1])
            AA.activation(out=flushd[:], in_=stats[:, 7:8], func=AF.Copy)
            AA.activation(out=flushd[:], in_=stats[:, 7:8], func=AF.Copy)
            AA.activation(out=pst[:, 0:8], in_=stats[:, 0:8], func=AF.Identity, scale=1.0 / 2048)
            AA.activation(out=pst[:, 0:8], in_=pst[:, 0:8], func=AF.Ln, bias=epsrms[:, 0:1])
            i = AA.activation(out=pst[:, 0:8], in_=pst[:, 0:8], func=AF.Exp, scale=-0.5)
            sac.inc(i)
            VV.wait_ge(sac.h, sac.n)
            for tcc in range(NTC):
                par = tcc % 2
                YY.wait_ge(sdv.h, x1r_free[par])
                i = YY.dma_start(out=x1r[par], in_=x1_dram[tcc * 128:(tcc + 1) * 128, :])
                sxr[par].inc(i, 16)
                VV.wait_ge(sxr[par].h, sxr[par].n)
                i = VV.scalar_tensor_tensor(out=rf(tcc), in0=y2[:, tcc * D:(tcc + 1) * D],
                                            scalar=pst[:, tcc:tcc + 1], in1=x1r[par],
                                            op0=ALU.mult, op1=ALU.add)
                sdv.inc(i)
                x1r_free[par] = sdv.n
            r2_dv = sdv.n
            AA.wait_ge(sdv.h, r2_dv)
            for tcc in range(NTC):
                AA.activation(out=obuf[0], in_=rf(tcc), func=AF.Square,
                              accum_out=stats[:, 8 + tcc:9 + tcc])
                AA.activation(out=obuf[0], in_=rf(tcc), func=AF.Identity,
                              accum_out=stats[:, 16 + tcc:17 + tcc])
            AA.activation(out=flushd[:], in_=stats[:, 15:16], func=AF.Copy)
            AA.activation(out=flushd[:], in_=stats[:, 23:24], func=AF.Copy)
            AA.activation(out=pst[:, 8:16], in_=stats[:, 16:24], func=AF.Identity, scale=1.0 / 2048)
            AA.activation(out=pst[:, 16:24], in_=stats[:, 8:16], func=AF.Identity, scale=1.0 / 2048)
            i = AA.activation(out=pst[:, 24:32], in_=pst[:, 8:16], func=AF.Square)
            sac.inc(i)
            VV.wait_ge(sac.h, sac.n)
            i = VV.scalar_tensor_tensor(out=pst[:, 32:40], in0=pst[:, 24:32], scalar=-1.0,
                                        in1=pst[:, 16:24], op0=ALU.mult, op1=ALU.add)
            sdv.inc(i)
            AA.wait_ge(sdv.h, sdv.n)
            AA.activation(out=pst[:, 40:48], in_=pst[:, 32:40], func=AF.Ln, bias=epsln[:, 0:1])
            i = AA.activation(out=pst[:, 48:56], in_=pst[:, 40:48], func=AF.Exp, scale=-0.5)
            sac.inc(i)
            VV.wait_ge(sac.h, sac.n)
            for tcc in range(NTC):
                par = tcc % 2
                VV.wait_ge(sout.h, obuf_free[par])
                i = VV.tensor_scalar(out=obuf[par], in0=rf(tcc), scalar1=pst[:, 8 + tcc:9 + tcc],
                                     scalar2=pst[:, 48 + tcc:49 + tcc],
                                     op0=ALU.subtract, op1=ALU.mult)
                sdv.inc(i)
                YY.wait_ge(sdv.h, sdv.n)
                i = YY.dma_start(out=y_out[tcc * 128:(tcc + 1) * 128, :], in_=obuf[par])
                sout.inc(i, 16)
                obuf_free[par] = sout.n
            YY.wait_ge(sout.h, sout.n)
            GG.wait_ge(sout.h, sout.n)
            _scope(None)

        block.sync(body)

    return nc


def kernel(L0_attn_hidden_states, L0_attn_wq, L0_attn_bq, L0_attn_wk, L0_attn_bk,
           L0_attn_wv, L0_attn_bv, L0_attn_wo, L0_attn_bo, L0_attn_wmap_q,
           L0_attn_wmap_k, L0_ffn_gate_up_packed, L0_ffn_gate_up_scales,
           L0_ffn_down_packed, L0_ffn_down_scales):
    x = np.asarray(L0_attn_hidden_states, dtype=np.float32)
    wq = np.asarray(L0_attn_wq, dtype=np.float32)
    wk = np.asarray(L0_attn_wk, dtype=np.float32)
    wv = np.asarray(L0_attn_wv, dtype=np.float32)
    wo = np.asarray(L0_attn_wo, dtype=np.float32)
    bq = np.asarray(L0_attn_bq, dtype=np.float32)
    bk = np.asarray(L0_attn_bk, dtype=np.float32)
    bv = np.asarray(L0_attn_bv, dtype=np.float32)
    bo = np.asarray(L0_attn_bo, dtype=np.float32)
    wmq = np.asarray(L0_attn_wmap_q, dtype=np.float32)
    wmk = np.asarray(L0_attn_wmap_k, dtype=np.float32)
    gup = np.asarray(L0_ffn_gate_up_packed, dtype=np.int32)
    gus = np.asarray(L0_ffn_gate_up_scales, dtype=np.float16)
    dnp = np.asarray(L0_ffn_down_packed, dtype=np.int32)
    dns = np.asarray(L0_ffn_down_scales, dtype=np.float16)

    wqT = np.ascontiguousarray(wq.T)
    wkT = np.ascontiguousarray(wk.T)
    wvT = np.ascontiguousarray(wv.T)
    woT = np.ascontiguousarray(wo.T)
    bqT = np.ascontiguousarray(bq.reshape(NH, 128).T)
    bkT = np.ascontiguousarray(bk.reshape(NH, 128).T)
    wmqT = np.ascontiguousarray(wmq.T)
    wmkT = np.ascontiguousarray(wmk.T)
    gupT = np.ascontiguousarray(gup.T)
    dnT = np.ascontiguousarray(dnp.T)

    sgu = []
    for wc in range(2):
        t = np.repeat(gus[8 * wc:8 * wc + 8], 16, axis=0)
        sgu.append(np.ascontiguousarray(np.repeat(t, 2, axis=1)))
    sdn = np.zeros((6, 128, 2 * D), dtype=np.float16)
    for wcp in range(6):
        r0, r1 = 8 * wcp, min(8 * wcp + 8, 44)
        t = np.repeat(dns[r0:r1], 16, axis=0)
        sdn[wcp, :t.shape[0], :] = np.repeat(t, 2, axis=1)

    ident = np.eye(128, dtype=np.float16)
    onesr = np.ones((1, 128), dtype=np.float16)

    in_maps = []
    for c in range(8):
        b, hf = c // 2, c % 2
        xs = np.ascontiguousarray(x[b, hf * TOK:(hf + 1) * TOK, :])
        m4 = np.zeros((64, 4), dtype=np.float32)
        m4[:, b] = 1.0
        in_maps.append(dict(
            x_in=xs, wqT=wqT, wkT=wkT, wvT=wvT, woT=woT,
            bqT=bqT, bkT=bkT, bv=bv.reshape(1, D), bo=bo.reshape(1, D),
            wmqT=wmqT, wmkT=wmkT, gupT=gupT, dnT=dnT,
            sgu0=sgu[0], sgu1=sgu[1], sdn=sdn,
            ident=ident, onesr=onesr, mask4p=m4,
        ))
    if _RET_MAPS[0]:
        return in_maps
    nc = build()
    res = run_bass_kernel_spmd(nc, in_maps, core_ids=list(range(8)))
    out = np.empty((B, S, D), dtype=np.float32)
    for c in range(8):
        b, hf = c // 2, c % 2
        out[b, hf * TOK:(hf + 1) * TOK, :] = res.results[c]["y"]
    return out
